# version 04
# baseline (speedup 1.0000x reference)
import math
from collections import defaultdict

import numpy as np

import concourse.bass as bass
import concourse.mybir as mybir
from concourse.bass_utils import run_bass_kernel_spmd

F32 = mybir.dt.float32
BF16 = mybir.dt.bfloat16
AF = mybir.ActivationFunctionType
ALU = mybir.AluOpType
AX = mybir.AxisListType

B, L, D = 4, 2048, 2048
NH, HD = 16, 128
DF = 8192
EPS = 1e-5
SCALE = HD ** -0.5
NT = L // 128
NOWN = 8
KC = [1, 1, 2, 2, 3, 3, 4, 4]
NEG = -1.0e30


class Op:
    __slots__ = ("eng", "fn", "deps", "inc", "idx", "after", "need", "qsnap")

    def __init__(self, eng, fn, deps, inc, idx):
        self.eng, self.fn, self.deps, self.inc, self.idx = eng, fn, deps, inc, idx
        self.after = 0
        self.need = inc == 16


class Res:
    __slots__ = ("w", "r")

    def __init__(self):
        self.w = None
        self.r = []


ENGS = ("sync", "gpsimd", "tensor", "vector", "scalar")


class Sched:
    def __init__(self):
        self.ops = []
        self.fence_deps = []
        self.R = defaultdict(Res)
        self.enabled = True

    def add(self, eng, fn, reads=(), writes=(), extra=(), dma=False):
        if not self.enabled:
            return None
        R = self.R
        deps = list(self.fence_deps)
        for name in reads:
            res = R[name]
            if res.w is not None:
                deps.append(res.w)
        for name in writes:
            res = R[name]
            if res.w is not None:
                deps.append(res.w)
            deps.extend(res.r)
        deps.extend(d for d in extra if d is not None)
        best = {}
        for d in deps:
            k = d.eng
            if k not in best or d.idx > best[k].idx:
                best[k] = d
        deps = list(best.values())
        op = Op(eng, fn, deps, 16 if dma else 1, len(self.ops))
        for d in deps:
            assert d.idx < op.idx
        self.ops.append(op)
        for name in reads:
            R[name].r.append(op)
        for name in writes:
            R[name].w = op
            R[name].r = []
        return op

    def fence(self):
        last = {}
        for op in self.ops:
            last[op.eng] = op
        self.fence_deps = list(last.values())

    def finalize(self):
        for op in self.ops:
            for d in op.deps:
                if d.eng != op.eng or d.inc == 16:
                    d.need = True
        cum = defaultdict(int)
        dma_cum = defaultdict(int)
        for op in self.ops:
            op.qsnap = dict(dma_cum)
            if op.need:
                cum[op.eng] += op.inc
                if op.inc == 16:
                    dma_cum[op.eng] = cum[op.eng]
            op.after = cum[op.eng]

    def emit_for(self, engname, eng, sems, sep_ap=None):
        hw = defaultdict(int)
        prev = None
        nsep = 0
        for op in self.ops:
            if op.eng != engname:
                continue
            for d in op.deps:
                if d.eng == engname and d.inc != 16:
                    continue
                val = op.qsnap.get(d.eng, 0) if d.inc == 16 else d.after
                assert val >= d.after or d.inc != 16
                if val > hw[d.eng]:
                    eng.wait_ge(sems[d.eng], val)
                    hw[d.eng] = val
            if engname == "gpsimd" and op.inc == 16:
                val = op.qsnap.get("gpsimd", 0)
                if val > hw["gpsimd"]:
                    eng.wait_ge(sems["gpsimd"], val)
                    hw["gpsimd"] = val
            if (sep_ap is not None and prev is not None
                    and any(d is prev for d in op.deps)):
                if engname == "scalar":
                    eng.mul(sep_ap, sep_ap, 1.0)
                else:
                    eng.tensor_scalar_mul(sep_ap, sep_ap, 1.0)
                nsep += 1
            if op.fn is not None:
                ins = op.fn(eng)
                if op.need:
                    ins.then_inc(sems[op.eng], op.inc)
                prev = op
        self.nsep = getattr(self, "nsep", 0) + nsep


def kc_of(g):
    return (g + 4) // 4


def _load_table_set(nc, want):
    from concourse.hw_specs import get_activation_tables
    for idx, funcs in enumerate(get_activation_tables(nc.m.arch).values()):
        if all(f in funcs for f in want):
            def emit(e, idx=idx):
                return e.add_instruction(mybir.InstLoadActFuncSet(
                    name=nc.get_next_instruction_name(),
                    act_func_set_id=idx, ins=[], outs=[]))
            return emit
    raise ValueError(f"no table set contains {want}")


def build(nphases=99, p2lvl=99, dump_sp=False, kinds="KVQG"):
    nc = bass.Bass("TRN2", target_bir_lowering=False,
                   detect_race_conditions=False)

    def din(name, shape):
        return nc.dram_tensor(name, shape, F32, kind="ExternalInput").ap()

    x_kv = din("x_kv", [L, D])
    x_own = din("x_own", [NOWN * 128, D])
    wqT = din("wqT", [8, 128, 16, 256])
    wkT = din("wkT", [8, 128, 16, 256])
    wvT = din("wvT", [8, 128, 16, 256])
    wgT = din("wgT", [8, 128, 16, 256])
    woT = din("woT", [8, 128, 16, 256])
    gateT = din("gateT", [64, 128, 16, 128])
    upT = din("upT", [64, 128, 16, 128])
    downT = din("downT", [2, 4, 8, 128, 2048])
    wmod = din("wmod", [D, 6 * D])
    c_fm = din("c_fm", [128, 16])
    nwrep_in = din("normw_rep", [128, 128])
    cos_k = din("cos_k", [128, L])
    sin_k = din("sin_k", [128, L])
    cos_q = din("cos_q", [128, NOWN * 128])
    sin_q = din("sin_q", [128, NOWN * 128])
    masks_in = din("masks", [128, NOWN, 512])
    id_in = din("ident", [128, 128])
    pmat_in = din("pmat", [128, 128])

    y_own = nc.dram_tensor("y_own", [NOWN * 128, D], F32,
                           kind="ExternalOutput").ap()
    spkind = "ExternalOutput" if dump_sp else "Internal"
    Vsp = nc.dram_tensor("Vsp", [NH, L, HD], BF16, kind=spkind).ap()
    Gsp = nc.dram_tensor("Gsp", [NH, NOWN * 128, HD], BF16, kind=spkind).ap()
    if dump_sp:
        dbg_x2 = nc.dram_tensor("dbg_x2", [NOWN * 128, D], F32,
                                kind="ExternalOutput").ap()
        dbg_h2 = nc.dram_tensor("dbg_h2", [128, 16, 1024], BF16,
                                kind="ExternalOutput").ap()
        dbg_gat = nc.dram_tensor("dbg_gat", [128, 16, 1024], BF16,
                                 kind="ExternalOutput").ap()
        dbg_gated = nc.dram_tensor("dbg_gated", [128, 32, 1024], BF16,
                                   kind="ExternalOutput").ap()
        dbg_h = nc.dram_tensor("dbg_h", [128, 16, 2048], BF16,
                               kind="ExternalOutput").ap()
        dbg_st = nc.dram_tensor("dbg_st", [128, 6, 16], F32,
                                kind="ExternalOutput").ap()
        dbg_K = nc.dram_tensor("dbg_K", [128, 16, 2048], BF16,
                               kind="ExternalOutput").ap()
        dbg_Q = nc.dram_tensor("dbg_Q", [128, 16, 1024], BF16,
                               kind="ExternalOutput").ap()
    rows_d = nc.dram_tensor("rows_d", [2, D], F32).ap()

    S = Sched()

    from contextlib import ExitStack
    with ExitStack() as ctx:
        def sb(name, shape, dt):
            return ctx.enter_context(nc.sbuf_tensor(name, shape, dt))

        arA = sb("arA", [128, 16, 2048], BF16)
        arB = sb("arB", [128, 16, 2048], BF16)
        arC = sb("arC", [128, 16, 1024], BF16)
        arD = sb("arD", [128, 16, 1024], BF16)
        w_sb = sb("w_sb", [128, 16, 256], BF16)
        bc_sb = sb("bc_sb", [128, 2, 512], F32)
        nwrep = sb("nwrep", [128, 128], F32)
        sm = sb("sm", [128, 128], F32)
        stats = sb("stats", [128, 6, 16], F32)
        stat2 = sb("stat2", [128, 3, 16], F32)
        modb = sb("modb", [128, 2, 96], F32)
        modc = sb("modc", [128, 2, 16], BF16)
        id_bf = sb("id_bf", [128, 128], BF16)
        pmat = sb("pmat_s", [128, 128], BF16)
        c_sb = sb("c_sb", [128, 16], F32)
        bias_sb = sb("bias_sb", [128, 1], F32)
        sepv = sb("sepv", [128, 1], F32)
        seps = sb("seps", [128, 1], F32)
        ps0 = ctx.enter_context(nc.psum_tensor("ps0", [128, 512], F32))
        ps1 = ctx.enter_context(nc.psum_tensor("ps1", [128, 512], F32))
        ps2 = ctx.enter_context(nc.psum_tensor("ps2", [128, 512], F32))
        ps3 = ctx.enter_context(nc.psum_tensor("ps3", [128, 512], F32))
        ps4 = ctx.enter_context(nc.psum_tensor("ps4", [128, 512], F32))
        ps5 = ctx.enter_context(nc.psum_tensor("ps5", [128, 512], F32))
        ps6 = ctx.enter_context(nc.psum_tensor("ps6", [128, 512], F32))
        ptr = ctx.enter_context(nc.psum_tensor("ptr", [128, 1024], BF16))
        sem_sync = ctx.enter_context(nc.semaphore("sem_sync"))
        sem_gps = ctx.enter_context(nc.semaphore("sem_gps"))
        sem_ten = ctx.enter_context(nc.semaphore("sem_ten"))
        sem_vec = ctx.enter_context(nc.semaphore("sem_vec"))
        sem_sca = ctx.enter_context(nc.semaphore("sem_sca"))
        sems = {"sync": sem_sync, "gpsimd": sem_gps, "tensor": sem_ten,
                "vector": sem_vec, "scalar": sem_sca}
        psb = [ps0, ps1, ps2, ps3, ps4, ps5, ps6]
        ptr_f32 = ptr[:].bitcast(F32)
        ps1_bf = ps1[:].bitcast(BF16)

        arA_fl = arA[:].rearrange("p a b -> p (a b)")
        arA_f32 = arA_fl.bitcast(F32)
        h_fm = arA
        wmod_f32 = arA_f32[:, 0:12288]
        scores_sb = arA_f32[:, 0:2048]
        e_sb = arA_f32[:, 2048:4096]
        sg_sb = arA_f32[:, 4096:6144]
        av_sb = arA_f32[:, 6144:8192].rearrange("p (a b) -> p a b", b=128)
        p_sb = arA_fl[:, 16384:18432]
        pT_sb = arA_fl[:, 18432:18944]
        pT_sb2 = arA_fl[:, 27648:28160]
        gtm_sb = arA_fl[:, 18944:19072]
        Gt_fl = arA_fl[:, 19456:21504]
        Gt = Gt_fl.rearrange("p (a b) -> p a b", b=128)
        v_fl = arA_fl[:, 21504:23552]
        v_sb = v_fl.rearrange("p (a b) -> p a b", b=128)
        mask_fl = arA_fl[:, 23552:27648]
        mask_sb = mask_fl.rearrange("p (a b) -> p a b", b=512)
        x3_sb = arA_f32[:, 0:2048]
        gated = arA_fl.rearrange("p (a b) -> p a b", b=1024)

        K_fm = arB
        x2 = arB[:].rearrange("p a b -> p (a b)").bitcast(F32).rearrange(
            "p (a b) -> p a b", b=2048)
        Q_own = arC
        h2_fm = arC

        arD_fl = arD[:].rearrange("p a b -> p (a b)")
        arD_f32 = arD_fl.bitcast(F32)
        x_sb = arD_f32[:, 0:2048]
        trashD = arD_f32[:, 2048:4096]
        xn_sb = arD_fl[:, 8192:10240]
        cosk_sb = arD_fl[:, 0:2048]
        sink_sb = arD_fl[:, 2048:4096]
        cosq_sb = arD_fl[:, 4096:5120]
        sinq_sb = arD_fl[:, 5120:6144]
        a_rp = arD_f32[:, 3072:3584]
        b_rp = arD_f32[:, 3584:4096]
        kraw = arD_fl[:, 8192:8704]
        w2_sb = arD_fl[:, 8704:12800].rearrange("p (a b) -> p a b", b=256)
        gat_fm = arD
        silu_t = arD_f32[:, 2048:2560]
        w5B = arD_fl[:, 8192:12288]
        silu_t2 = arD_f32[:, 2560:3072]

        w_fl = w_sb[:].rearrange("p a b -> p (a b)")
        vstage = bc_sb[:].rearrange("p a b -> p (a b)").bitcast(BF16)

        hK = "hK"

        def dma(eng, out, in_, reads=(), writes=(), extra=()):
            return S.add(eng, lambda e, o=out, i=in_: e.dma_start(out=o, in_=i),
                         reads=reads, writes=writes, extra=extra, dma=True)

        def vop(fn, reads=(), writes=(), eng="vector", extra=()):
            return S.add(eng, fn, reads=reads, writes=writes, extra=extra)

        S.add("vector", lambda e: e.memset(sepv[:], 0.0))
        S.add("scalar", lambda e: e.memzero(seps[:]))
        dma("sync", c_sb[:], c_fm, writes=["c"])
        dma("sync", nwrep[:], nwrep_in, writes=["nw"])
        dma("gpsimd", id_bf[:], id_in, writes=["id"])
        dma("gpsimd", pmat[:], pmat_in, writes=["pm"])
        S.add("scalar", _load_table_set(nc, (AF.Exp, AF.Ln, AF.Square)),
              writes=["tab"])
        S.add("scalar", lambda e: e.activation(stats[:, 0, :], c_sb[:], AF.Exp,
                                               scale=-1.0),
              reads=["c", "tab"], writes=["st"])
        vop(lambda e: e.tensor_scalar_add(stats[:, 0, :], stats[:, 0, :], 1.0),
            reads=["st"], writes=["st"])
        vop(lambda e: e.reciprocal(stats[:, 1, :], stats[:, 0, :]),
            reads=["st"], writes=["st"])
        vop(lambda e: e.tensor_mul(c_sb[:], c_sb[:], stats[:, 1, :]),
            reads=["st", "c"], writes=["sc"])

        first = True
        for k in range(16):
            dma("sync", wmod_f32, wmod[k * 128:(k + 1) * 128, :],
                writes=["wmod"])
            for j in range(96):
                st_flag = first
                first = False
                S.add("tensor",
                      lambda e, k=k, j=j, s=st_flag: e.matmul(
                          ps0[:, j:j + 1], wmod_f32[:, j * 128:(j + 1) * 128],
                          c_sb[:, k:k + 1], start=s,
                          stop=(k == 15 and j == 95), skip_group_check=True),
                      reads=["wmod", "sc"], writes=["b0"])
        vop(lambda e: e.tensor_copy(modb[:, 0, :], ps0[:, 0:96]),
            reads=["b0"], writes=["mraw"])

        vop(lambda e: e.tensor_copy(modb[:, 1, 0:16], modb[:, 0, 0:16]),
            reads=["mraw"], writes=["mfm"])
        vop(lambda e: e.tensor_scalar_add(modb[:, 1, 16:32], modb[:, 0, 16:32], 1.0),
            reads=["mraw"], writes=["mfm"])
        vop(lambda e: e.tensor_copy(modb[:, 1, 48:64], modb[:, 0, 48:64]),
            reads=["mraw"], writes=["mfm"])
        vop(lambda e: e.tensor_scalar_add(modb[:, 1, 64:80], modb[:, 0, 64:80], 1.0),
            reads=["mraw"], writes=["mfm"])
        for seg in (32, 80):
            vop(lambda e, s=seg: e.tensor_scalar_max(stats[:, 0, :],
                                                     modb[:, 0, s:s + 16], 0.25),
                reads=["mraw"], writes=["st"])
            vop(lambda e, s=seg: e.tensor_scalar_min(stats[:, 1, :],
                                                     modb[:, 0, s:s + 16], -0.25),
                reads=["mraw"], writes=["st"])
            vop(lambda e, s=seg: e.tensor_scalar(stats[:, 2, :],
                                                 modb[:, 0, s:s + 16], 0.0, None,
                                                 ALU.is_ge),
                reads=["mraw"], writes=["st"])
            vop(lambda e: e.tensor_sub(stats[:, 3, :], stats[:, 0, :],
                                       stats[:, 1, :]),
                reads=["st"], writes=["st"])
            vop(lambda e: e.tensor_mul(stats[:, 3, :], stats[:, 3, :],
                                       stats[:, 2, :]),
                reads=["st"], writes=["st"])
            vop(lambda e, s=seg: e.tensor_add(modb[:, 1, s:s + 16],
                                              stats[:, 1, :], stats[:, 3, :]),
                reads=["st"], writes=["mfm"])
        for i, seg in enumerate((32, 80)):
            vop(lambda e, i=i, s=seg: e.tensor_copy(modc[:, i, :],
                                                    modb[:, 1, s:s + 16]),
                reads=["mfm"], writes=["modc"])
            S.add("tensor",
                  lambda e, i=i: e.transpose(ptr[0:16, 0:128], modc[:, i, :],
                                             id_bf[:]),
                  reads=["modc", "id"], writes=["b7"])
            vop(lambda e, i=i: e.tensor_copy(bc_sb[0:16, i, 0:128],
                                             ptr[0:16, 0:128]),
                reads=["b7"], writes=["garow"])
        dma("sync", rows_d[0, :], bc_sb[0:16, 0, 0:128], reads=["garow"], writes=["rows"])
        dma("sync", rows_d[1, :], bc_sb[0:16, 1, 0:128], reads=["garow"], writes=["rows"])

        def modcol(base, k):
            return modb[:, 1, base + k:base + k + 1]

        S.fence()

        def emit_ln(x_tile, x_res, bb, opc_base, sft_base, out_t, out_col, out_res):
            S.add("scalar",
                  lambda e, x_tile=x_tile, bb=bb: e.activation(
                      trashD, x_tile, AF.Square,
                      accum_out=stats[:, 1, bb:bb + 1]),
                  reads=[x_res], writes=["trD", "st"])
            vop(lambda e, x_tile=x_tile, bb=bb: e.reduce_sum(
                stats[:, 0, bb:bb + 1], x_tile, AX.X),
                reads=[x_res], writes=["st"])
            vop(lambda e, bb=bb: e.tensor_scalar_mul(
                stats[:, 2, bb:bb + 1], stats[:, 0, bb:bb + 1], 1.0 / D),
                reads=["st"], writes=["st"])
            vop(lambda e, bb=bb: e.tensor_mul(
                stats[:, 5, bb:bb + 1], stats[:, 2, bb:bb + 1],
                stats[:, 2, bb:bb + 1]),
                reads=["st"], writes=["st"])
            vop(lambda e, bb=bb: e.tensor_scalar(
                stats[:, 4, bb:bb + 1], stats[:, 1, bb:bb + 1], 1.0 / D, EPS,
                ALU.mult, ALU.add),
                reads=["st"], writes=["st"])
            vop(lambda e, bb=bb: e.tensor_sub(
                stats[:, 4, bb:bb + 1], stats[:, 4, bb:bb + 1],
                stats[:, 5, bb:bb + 1]),
                reads=["st"], writes=["st"])
            S.add("scalar",
                  lambda e, bb=bb: e.activation(stats[:, 5, bb:bb + 1],
                                                stats[:, 4, bb:bb + 1], AF.Ln),
                  reads=["st"], writes=["st"])
            S.add("scalar",
                  lambda e, bb=bb: e.activation(stats[:, 3, bb:bb + 1],
                                                stats[:, 5, bb:bb + 1], AF.Exp,
                                                scale=-0.5),
                  reads=["st"], writes=["st"])
            vop(lambda e, x_tile=x_tile, bb=bb: e.tensor_scalar(
                xn_sb, x_tile, stats[:, 2, bb:bb + 1], stats[:, 3, bb:bb + 1],
                ALU.subtract, ALU.mult),
                reads=[x_res, "st"], writes=["xn"])
            for half in range(2):
                for kk in range(8):
                    k = half * 8 + kk
                    S.add("tensor",
                          lambda e, k=k, kk=kk: e.transpose(
                              ptr[:, kk * 128:(kk + 1) * 128],
                              xn_sb[:, k * 128:(k + 1) * 128], id_bf[:]),
                          reads=["xn", "id"], writes=["b7"])
                for kk in range(8):
                    k = half * 8 + kk
                    vop(lambda e, k=k, kk=kk, out_t=out_t, out_col=out_col,
                        ob=opc_base, sb=sft_base: e.tensor_scalar(
                            out_t[:, k, out_col:out_col + 128],
                            ptr[:, kk * 128:(kk + 1) * 128],
                            modcol(ob, k), modcol(sb, k), ALU.mult, ALU.add),
                        reads=["b7", "mfm"], writes=[out_res])

        S.enabled = nphases >= 2
        for blk in range(NT):
            dma("sync", x_sb, x_kv[blk * 128:(blk + 1) * 128, :], writes=["xs"])
            emit_ln(x_sb, "xs", blk % 16, 16, 0, h_fm, blk * 128, hK)

        S.fence()
        if dump_sp:
            dma("sync", dbg_h, arA[:], reads=[hK], writes=["dbghh"])
            dma("sync", dbg_st, stats[:], reads=["st"], writes=["dbgst"])

        S.enabled = nphases >= 3
        dma("gpsimd", cosk_sb, cos_k, writes=["tabs"])
        dma("gpsimd", sink_sb, sin_k, writes=["tabs"])
        dma("gpsimd", cosq_sb, cos_q, writes=["tabs"])
        dma("gpsimd", sinq_sb, sin_q, writes=["tabs"])

        hAP = h_fm[:]
        pstep_h = hAP.ap[0][0]

        def own_rhs(k, oc):
            off = k * 2048 + 1024 * oc
            return bass.AP(arA, off, [[pstep_h, 128], [256, 4], [1, 128]])

        def rope_evac(pbank, pres, cos_t, sin_t, col0, dst, wname):
            vop(lambda e, pbank=pbank: e.tensor_copy(kraw, pbank[:, 0:512]),
                reads=[pres], writes=["kraw"])
            vop(lambda e, cos_t=cos_t, col0=col0, pbank=pbank: e.tensor_mul(
                a_rp, pbank[:, 0:512], cos_t[:, col0:col0 + 512]),
                reads=[pres, "tabs"], writes=["arp"])
            S.add("tensor",
                  lambda e: e.matmul(ps2[:, 0:512], pmat[:], kraw,
                                     start=True, stop=True),
                  reads=["kraw", "pm"], writes=["b2"])
            vop(lambda e, sin_t=sin_t, col0=col0: e.tensor_mul(
                b_rp, ps2[:, 0:512], sin_t[:, col0:col0 + 512]),
                reads=["b2", "tabs"], writes=["brp"])
            vop(lambda e, dst=dst: e.tensor_add(dst, a_rp, b_rp),
                reads=["arp", "brp"], writes=[wname])

        wcnt = 0
        for wT, kind in ((wkT, "K"), (wvT, "V"), (wqT, "Q"), (wgT, "G")):
            if kind not in kinds:
                continue
            for fc in range(8):
                wcnt += 1
                wb = w_sb[:] if wcnt % 2 == 0 else w2_sb
                wres = "w%d" % (wcnt % 2)
                dma("gpsimd", wb, wT[fc], writes=[wres])
                if kind in ("K", "Q"):
                    for hl in range(2):
                        h = fc * 2 + hl
                        ncols = 4 if kind == "K" else 2
                        for tc in range(ncols):
                            par = (hl * 4 + tc) % 2
                            pbank = ps0 if par == 0 else ps4
                            pres = "b0" if par == 0 else "b4"
                            for k in range(16):
                                rhs = (h_fm[:, k, tc * 512:(tc + 1) * 512]
                                       if kind == "K" else own_rhs(k, tc))
                                S.add("tensor",
                                      lambda e, k=k, hl=hl, rhs=rhs,
                                      pbank=pbank, wb=wb: e.matmul(
                                          pbank[:, 0:512],
                                          wb[:, k, hl * 128:(hl + 1) * 128],
                                          rhs, start=(k == 0), stop=(k == 15)),
                                      reads=[wres, hK], writes=[pres])
                            if kind == "K":
                                rope_evac(pbank, pres, cosk_sb, sink_sb,
                                          tc * 512,
                                          K_fm[:, h, tc * 512:(tc + 1) * 512],
                                          "Kf")
                            else:
                                rope_evac(pbank, pres, cosq_sb, sinq_sb,
                                          tc * 512,
                                          Q_own[:, h, tc * 512:(tc + 1) * 512],
                                          "Qf")
                else:
                    nblk = 16 if kind == "V" else NOWN
                    spname = "Vsp" if kind == "V" else "Gsp"
                    dstT = Vsp if kind == "V" else Gsp
                    seqlen = L if kind == "V" else NOWN * 128
                    for tb in range(nblk):
                        lo = tb * 128 if kind == "V" else 256 * tb
                        pbank = ps0 if tb % 2 == 0 else ps4
                        pres = "b0" if tb % 2 == 0 else "b4"
                        for k in range(16):
                            S.add("tensor",
                                  lambda e, k=k, lo=lo, pbank=pbank,
                                  wb=wb: e.matmul(
                                      pbank[:, 0:256],
                                      bass.AP(arA, k * 2048 + lo,
                                              [[pstep_h, 128], [1, 128]]),
                                      wb[:, k, :], start=(k == 0),
                                      stop=(k == 15)),
                                  reads=[wres, hK], writes=[pres])
                        vop(lambda e, tb=tb, pbank=pbank: e.tensor_copy(
                            vstage[:, (tb % 4) * 512:(tb % 4) * 512 + 256],
                            pbank[:, 0:256]),
                            reads=[pres], writes=["vst"])
                        if tb % 4 == 3 or tb == nblk - 1:
                            nb = tb % 4 + 1
                            tb0 = tb - nb + 1
                            for hh in range(2):
                                dst_ap = bass.AP(
                                    dstT.tensor,
                                    (2 * fc + hh) * seqlen * 128 + tb0 * 128 * 128,
                                    [[128, 128], [128 * 128, nb], [1, 128]])
                                src_ap = bass.AP(
                                    vstage.tensor,
                                    vstage.offset + hh * 128,
                                    [list(vstage.ap[0]), [512, nb], [1, 128]])
                                dma("sync", dst_ap, src_ap, reads=["vst"],
                                    writes=[spname])

        S.fence()
        if dump_sp:
            dma("sync", dbg_K, arB[:], reads=["Kf"], writes=["dbgk"])
            dma("sync", dbg_Q, arC[:], reads=["Qf"], writes=["dbgq"])

        S.enabled = nphases >= 4
        dma("gpsimd", mask_fl, masks_in.rearrange("p a c -> p (a c)"),
            writes=["mask"])
        for i in range(NOWN):
            kc = KC[i]
            dma("sync", Gt_fl,
                bass.AP(Gsp.tensor, i * 128 * 128,
                        [[128, 128], [NOWN * 128 * 128, 16], [1, 128]]),
                reads=["Gsp"], writes=["Gt"])
            S.add("scalar", lambda e: e.activation(e_sb, Gt_fl, AF.Exp,
                                                   scale=-1.0),
                  reads=["Gt", "tab"], writes=["es"])
            vop(lambda e: e.tensor_scalar_add(e_sb, e_sb, 1.0),
                reads=["es"], writes=["es"])
            vop(lambda e: e.reciprocal(e_sb, e_sb), reads=["es"], writes=["es"])
            vop(lambda e: e.tensor_mul(sg_sb, Gt_fl, e_sb),
                reads=["es", "Gt"], writes=["sg"])
            for h in range(NH):
                if p2lvl < 2:
                    continue
                dma("sync", v_fl[:, 0:kc * 512],
                    bass.AP(Vsp.tensor, h * L * 128,
                            [[128, 128], [128 * 128, kc * 4], [1, 128]]),
                    reads=["Vsp"], writes=["vv"])
                for j in range(kc):
                    sbank = ps0 if j % 2 == 0 else ps4
                    sres = "b0" if j % 2 == 0 else "b4"
                    S.add("tensor",
                          lambda e, h=h, i=i, j=j, sbank=sbank: e.matmul(
                              sbank[:, 0:512], Q_own[:, h, i * 128:(i + 1) * 128],
                              K_fm[:, h, j * 512:(j + 1) * 512],
                              start=True, stop=True),
                          reads=["Qf", "Kf"], writes=[sres])
                    if j == kc - 1:
                        vop(lambda e, j=j, i=i, sbank=sbank: e.tensor_add(
                            scores_sb[:, j * 512:(j + 1) * 512],
                            sbank[:, 0:512], mask_sb[:, i, :]),
                            reads=[sres, "mask"], writes=["sco"])
                    else:
                        vop(lambda e, j=j, sbank=sbank: e.tensor_copy(
                            scores_sb[:, j * 512:(j + 1) * 512], sbank[:, 0:512]),
                            reads=[sres], writes=["sco"])
                if p2lvl < 3:
                    continue
                vop(lambda e, kc=kc: e.reduce_max(
                    bias_sb[:], scores_sb[:, 0:kc * 512], AX.X, negate=True),
                    reads=["sco"], writes=["bi"])
                vop(lambda e: e.tensor_scalar_mul(bias_sb[:], bias_sb[:], SCALE),
                    reads=["bi"], writes=["bi"])
                S.add("scalar",
                      lambda e, kc=kc, h=h: e.activation(
                          p_sb[:, 0:kc * 512], scores_sb[:, 0:kc * 512],
                          AF.Exp, bias=bias_sb[:], scale=SCALE,
                          accum_out=stat2[:, 0, h:h + 1]),
                      reads=["sco", "bi", "tab"], writes=["pp", "l2"])
                if p2lvl < 5:
                    continue
                for j in range(kc):
                    tb_ap = ptr if j % 2 == 0 else ps1_bf
                    tb_res = "b7" if j % 2 == 0 else "b1"
                    pT_dst = pT_sb if j % 2 == 0 else pT_sb2
                    pT_res = "pT" if j % 2 == 0 else "pT2"
                    for s4 in range(4):
                        S.add("tensor",
                              lambda e, j=j, s4=s4, tb_ap=tb_ap: e.transpose(
                                  tb_ap[:, s4 * 128:(s4 + 1) * 128],
                                  p_sb[:, j * 512 + s4 * 128:
                                       j * 512 + (s4 + 1) * 128], id_bf[:]),
                              reads=["pp", "id"], writes=[tb_res])
                    vop(lambda e, tb_ap=tb_ap, pT_dst=pT_dst: e.tensor_copy(
                        pT_dst, tb_ap[:, 0:512]),
                        reads=[tb_res], writes=[pT_res])
                    for s4 in range(4):
                        S.add("tensor",
                              lambda e, j=j, s4=s4, kc=kc, pT_dst=pT_dst: e.matmul(
                                  ps2[:, 0:128],
                                  pT_dst[:, s4 * 128:(s4 + 1) * 128],
                                  v_sb[:, j * 4 + s4, :],
                                  start=(j == 0 and s4 == 0),
                                  stop=(j == kc - 1 and s4 == 3)),
                              reads=[pT_res, "vv"], writes=["b2"])
                if p2lvl < 6:
                    continue
                sq_op = S.add("scalar",
                              lambda e, h=h: e.activation(
                                  bc_sb[:, 1, 0:128], ps2[:, 0:128], AF.Square,
                                  accum_out=stat2[:, 1, h:h + 1]),
                              reads=["b2"], writes=["sm1", "sq2"])
                vop(lambda e, h=h: e.tensor_copy(av_sb[:, h, :], ps2[:, 0:128]),
                    reads=["b2"], writes=["av"], extra=[sq_op])
            if p2lvl < 7:
                continue
            vop(lambda e: e.tensor_mul(stats[:, 0, :], stat2[:, 0, :],
                                       stat2[:, 0, :]),
                reads=["l2"], writes=["st"])
            vop(lambda e: e.tensor_scalar_mul(stats[:, 1, :], stat2[:, 1, :],
                                              1.0 / HD),
                reads=["sq2"], writes=["st"])
            vop(lambda e: e.scalar_tensor_tensor(stats[:, 1, :], stats[:, 0, :],
                                                 EPS, stats[:, 1, :],
                                                 ALU.mult, ALU.add),
                reads=["st"], writes=["st"])
            S.add("scalar", lambda e: e.activation(stats[:, 0, :],
                                                   stats[:, 1, :], AF.Ln),
                  reads=["st"], writes=["st"])
            S.add("scalar", lambda e: e.activation(stat2[:, 2, :],
                                                   stats[:, 0, :], AF.Exp,
                                                   scale=-0.5),
                  reads=["st"], writes=["al"])
            for hq in range(4):
                for hl in range(4):
                    h = hq * 4 + hl
                    vop(lambda e, h=h: e.tensor_mul(
                        sm[:, :], sg_sb[:, h * 128:(h + 1) * 128], nwrep[:]),
                        reads=["sg", "nw"], writes=["sm0"])
                    vop(lambda e, h=h: e.scalar_tensor_tensor(
                        gtm_sb, av_sb[:, h, :], stat2[:, 2, h:h + 1],
                        sm[:, :], ALU.mult, ALU.mult),
                        reads=["av", "al", "sm0"], writes=["gtm"])
                    S.add("tensor",
                          lambda e, hl=hl: e.transpose(
                              ptr[:, hl * 128:(hl + 1) * 128], gtm_sb,
                              id_bf[:]),
                          reads=["gtm", "id"], writes=["b7"])
                vop(lambda e, hq=hq, i=i: e.tensor_copy(
                    bass.AP(arD, hq * 4 * 1024 + i * 128,
                            [[16 * 1024, 128], [1024, 4], [1, 128]]),
                    ptr[:, 0:512].rearrange("p (a b) -> p a b", b=128)),
                    reads=["b7"], writes=["gat"])

        S.fence()
        if dump_sp:
            dma("sync", dbg_gat, arD[:], reads=["gat"], writes=["dbgg"])

        S.enabled = nphases >= 5
        for dc in range(8):
            dma("sync", bc_sb[:, 0, 0:256],
                bass.AP(rows_d.tensor, dc * 256, [[0, 128], [1, 256]]),
                reads=["rows"], writes=["bc0"])
            dma("gpsimd", w_sb[:], woT[dc], writes=["w"])
            for ot in range(NOWN):
                pbank = ps0 if ot % 2 == 0 else ps4
                pres = "b0" if ot % 2 == 0 else "b4"
                for k in range(16):
                    S.add("tensor",
                          lambda e, k=k, ot=ot, pbank=pbank: e.matmul(
                              pbank[:, 0:256],
                              bass.AP(arD, k * 1024 + ot * 128,
                                      [[16 * 1024, 128], [1, 128]]),
                              w_sb[:, k, :], start=(k == 0), stop=(k == 15)),
                          reads=["w", "gat"], writes=[pres])
                vop(lambda e, ot=ot, dc=dc, pbank=pbank: e.tensor_mul(
                    x2[:, ot, dc * 256:(dc + 1) * 256], pbank[:, 0:256],
                    bc_sb[:, 0, 0:256]),
                    reads=[pres, "bc0"], writes=["x2"])
        for ot in range(NOWN):
            dma("sync", x3_sb, x_own[ot * 128:(ot + 1) * 128, :], writes=["x3"])
            vop(lambda e, ot=ot: e.tensor_add(x2[:, ot, :], x2[:, ot, :], x3_sb),
                reads=["x3"], writes=["x2"])
        S.fence()
        if dump_sp:
            for ot in range(NOWN):
                dma("sync", dbg_x2[ot * 128:(ot + 1) * 128, :], x2[:, ot, :],
                    reads=["x2"], writes=["dbgx"])

        S.enabled = nphases >= 6
        for ot in range(NOWN):
            emit_ln(x2[:, ot, :], "x2", ot, 64, 48, h2_fm, ot * 128, "h2")

        S.fence()

        if dump_sp:
            dma("sync", dbg_h2, arC[:], reads=["h2"], writes=["dbgh"])

        S.enabled = nphases >= 7
        S.add("scalar", _load_table_set(nc, (AF.Silu,)), writes=["tab5"])
        for half in range(2):
            for fc in range(32):
                ci = half * 32 + fc
                par = fc % 2
                gsl = w_fl[:, 0:2048] if par == 0 else w5B[:, 0:2048]
                usl = w_fl[:, 2048:4096] if par == 0 else w5B[:, 2048:4096]
                gres, ures = "wg%d" % par, "wu%d" % par
                dma("gpsimd", gsl.rearrange("p (a f) -> p a f", f=128),
                    gateT[ci], writes=[gres])
                dma("gpsimd", usl.rearrange("p (a f) -> p a f", f=128),
                    upT[ci], writes=[ures])
                for tc in range(2):
                    pg = psb[tc]
                    pu = psb[2 + tc]
                    for k in range(16):
                        S.add("tensor",
                              lambda e, k=k, tc=tc, pg=pg, gsl=gsl: e.matmul(
                                  pg[:, 0:512],
                                  gsl[:, k * 128:(k + 1) * 128],
                                  h2_fm[:, k, tc * 512:(tc + 1) * 512],
                                  start=(k == 0), stop=(k == 15)),
                              reads=[gres, "h2"], writes=["b%d" % tc])
                    for k in range(16):
                        S.add("tensor",
                              lambda e, k=k, tc=tc, pu=pu, usl=usl: e.matmul(
                                  pu[:, 0:512],
                                  usl[:, k * 128:(k + 1) * 128],
                                  h2_fm[:, k, tc * 512:(tc + 1) * 512],
                                  start=(k == 0), stop=(k == 15)),
                              reads=[ures, "h2"], writes=["b%d" % (2 + tc)])
                    sit = "sit%d" % tc
                    sdst = silu_t if tc == 0 else silu_t2
                    S.add("scalar",
                          lambda e, tc=tc, pg=pg, sdst=sdst: e.activation(
                              sdst, pg[:, 0:512], AF.Silu),
                          reads=["b%d" % tc, "tab5"], writes=[sit])
                    vop(lambda e, fc=fc, tc=tc, pu=pu, sdst=sdst: e.tensor_mul(
                        gated[:, fc, tc * 512:(tc + 1) * 512], sdst,
                        pu[:, 0:512]),
                        reads=[sit, "b%d" % (2 + tc)], writes=["gated"])
            if dump_sp and half == 0:
                dma("sync", dbg_gated, gated, reads=["gated"], writes=["dbgd"])
            for dc in range(4):
                dma("sync", bc_sb[:, 0, :],
                    bass.AP(rows_d.tensor, D + dc * 512, [[0, 128], [1, 512]]),
                    reads=["rows"], writes=["bc0"])
                for kp in range(8):
                    slot = (kp % 2) * 2048
                    wres = "wd%d" % (kp % 2)
                    dma("gpsimd", w_fl[:, slot:slot + 2048],
                        downT[half, dc, kp], writes=[wres])
                    for ot in range(NOWN):
                        pd = psb[ot] if ot < 7 else ptr_f32
                        for k in range(4):
                            kc32 = kp * 4 + k
                            S.add("tensor",
                                  lambda e, k=k, kc32=kc32, ot=ot, pd=pd,
                                  slot=slot: e.matmul(
                                      pd[:, 0:512],
                                      gated[:, kc32, ot * 128:(ot + 1) * 128],
                                      w_fl[:, slot + k * 512:
                                           slot + (k + 1) * 512],
                                      start=(kc32 == 0), stop=(kc32 == 31)),
                                  reads=[wres, "gated"],
                                  writes=["b%d" % min(ot, 7)])
                for ot in range(NOWN):
                    pd = psb[ot] if ot < 7 else ptr_f32
                    vop(lambda e, ot=ot, pd=pd: e.tensor_mul(
                        bc_sb[:, 1, :], pd[:, 0:512], bc_sb[:, 0, :]),
                        reads=["b%d" % min(ot, 7), "bc0"], writes=["bc1"])
                    vop(lambda e, ot=ot, dc=dc: e.tensor_add(
                        x2[:, ot, dc * 512:(dc + 1) * 512],
                        x2[:, ot, dc * 512:(dc + 1) * 512], bc_sb[:, 1, :]),
                        reads=["bc1"], writes=["x2"])

        S.enabled = True
        S.fence()
        outs = []
        for ot in range(NOWN):
            outs.append(dma("sync", y_own[ot * 128:(ot + 1) * 128, :],
                            x2[:, ot, :], reads=["x2"], writes=["y"]))
        S.add("gpsimd", None, extra=outs)

        S.finalize()
        with nc.Block() as blk:
            @blk.sync
            def _(eng):
                S.emit_for("sync", eng, sems)

            @blk.gpsimd
            def _(eng):
                S.emit_for("gpsimd", eng, sems)

            @blk.tensor
            def _(eng):
                S.emit_for("tensor", eng, sems)

            @blk.vector
            def _(eng):
                S.emit_for("vector", eng, sems, sep_ap=sepv[:])

            @blk.scalar
            def _(eng):
                S.emit_for("scalar", eng, sems, sep_ap=seps[:])

    return nc, len(S.ops)


_CACHE = {}


def _get_program():
    if "nc" not in _CACHE:
        _CACHE["nc"] = build()[0]
    return _CACHE["nc"]


def _rope_tables():
    d = HD
    inv = 1.0 / (10000.0 ** (np.arange(0, d, 2, dtype=np.float64) / d))
    t = np.arange(L, dtype=np.float64)
    fr = np.outer(inv, t)
    cos = np.cos(fr)
    sin = np.sin(fr)
    cos_full = np.concatenate([cos, cos], axis=0)
    sin_full = np.concatenate([sin, sin], axis=0)
    return cos_full.astype(np.float32), sin_full.astype(np.float32)


def kernel(L0_attn_hidden_states, L0_attn_wq, L0_attn_wk, L0_attn_wv,
           L0_attn_wg, L0_attn_wo, L0_attn_norm_w, L0_ffn_w_gate,
           L0_ffn_w_up, L0_ffn_w_down, L0_c, L0_wmod):
    x = np.ascontiguousarray(np.asarray(L0_attn_hidden_states, dtype=np.float32))
    wqT = np.ascontiguousarray(np.asarray(L0_attn_wq, np.float32).T)
    wkT = np.ascontiguousarray(np.asarray(L0_attn_wk, np.float32).T)
    wvT = np.ascontiguousarray(np.asarray(L0_attn_wv, np.float32).T)
    wgT = np.ascontiguousarray(np.asarray(L0_attn_wg, np.float32).T)
    woT = np.ascontiguousarray(np.asarray(L0_attn_wo, np.float32).T)
    gateT = np.ascontiguousarray(np.asarray(L0_ffn_w_gate, np.float32).T)
    upT = np.ascontiguousarray(np.asarray(L0_ffn_w_up, np.float32).T)
    downT = np.ascontiguousarray(np.asarray(L0_ffn_w_down, np.float32).T)
    wmod = np.ascontiguousarray(np.asarray(L0_wmod, np.float32))
    c = np.asarray(L0_c, np.float32)
    nw = np.asarray(L0_attn_norm_w, np.float32)

    nc = _get_program()
    in_maps = _make_in_maps(x, wqT, wkT, wvT, wgT, woT, gateT, upT, downT,
                            wmod, c, nw)

    res = run_bass_kernel_spmd(nc, in_maps, core_ids=list(range(8)))

    out = np.empty((B, L, D), dtype=np.float32)
    for core in range(8):
        b = core // 2
        r = core % 2
        y = res.results[core]["y_own"]
        for i in range(NOWN):
            g = 2 * i + r
            out[b, 128 * g:128 * g + 128, :] = y[128 * i:128 * i + 128, :]
    return out


def _tile_w2(wT):
    D_, F_ = wT.shape
    t = wT.reshape(16, 128, F_ // 256, 256)
    return np.ascontiguousarray(t.transpose(2, 1, 0, 3))


def _tile_g(wT):
    D_, F_ = wT.shape
    t = wT.reshape(16, 128, F_ // 128, 128)
    return np.ascontiguousarray(t.transpose(2, 1, 0, 3))


def _tile_d(dT):
    out = np.empty((2, 4, 8, 128, 2048), dtype=np.float32)
    for half in range(2):
        for dc in range(4):
            for kp in range(8):
                blk = dT[half * 4096 + kp * 512: half * 4096 + (kp + 1) * 512,
                         dc * 512:(dc + 1) * 512]
                out[half, dc, kp] = (blk.reshape(4, 128, 512)
                                     .transpose(1, 0, 2).reshape(128, 2048))
    return out


def _make_in_maps(x, wqT, wkT, wvT, wgT, woT, gateT, upT, downT, wmod, c, nw):
    cos_full, sin_full = _rope_tables()
    ident = np.eye(128, dtype=np.float32)
    pmat = np.zeros((128, 128), dtype=np.float32)
    for dd in range(64):
        pmat[dd + 64, dd] = -1.0
        pmat[dd, dd + 64] = 1.0
    nwrep = np.ascontiguousarray(np.broadcast_to(nw[None, :], (128, 128)))

    wq_t = _tile_w2(wqT)
    wk_t = _tile_w2(wkT)
    wv_t = _tile_w2(wvT)
    wg_t = _tile_w2(wgT)
    wo_t = _tile_w2(woT)
    gate_t = _tile_g(gateT)
    up_t = _tile_g(upT)
    down_t = _tile_d(downT)

    in_maps = []
    for core in range(8):
        b = core // 2
        r = core % 2
        own_tiles = [2 * i + r for i in range(NOWN)]
        own_idx = np.concatenate([np.arange(128 * g, 128 * g + 128)
                                  for g in own_tiles])
        perm = np.arange(L)
        if r == 1:
            perm = perm.reshape(NT // 2, 2, 128)[:, ::-1, :].reshape(L)
        inv_kpos = perm
        c_fm = np.ascontiguousarray(c[b].reshape(16, 128).T)
        masks = np.empty((NOWN, 128, 512), dtype=np.float32)
        for i, g in enumerate(own_tiles):
            kb = 512 * (kc_of(g) - 1)
            q_glob = 128 * g + np.arange(128)[:, None]
            k_prog = kb + np.arange(512)[None, :]
            k_glob = inv_kpos[np.minimum(k_prog, L - 1)]
            masks[i] = np.where(k_glob <= q_glob, 0.0, NEG)
        masks = np.ascontiguousarray(masks.transpose(1, 0, 2))
        in_maps.append({
            "x_kv": np.ascontiguousarray(x[b][inv_kpos]),
            "x_own": np.ascontiguousarray(x[b][own_idx]),
            "wqT": wq_t, "wkT": wk_t, "wvT": wv_t, "wgT": wg_t, "woT": wo_t,
            "gateT": gate_t, "upT": up_t, "downT": down_t, "wmod": wmod,
            "c_fm": c_fm, "normw_rep": nwrep,
            "cos_k": np.ascontiguousarray(cos_full[:, inv_kpos]),
            "sin_k": np.ascontiguousarray(sin_full[:, inv_kpos]),
            "cos_q": np.ascontiguousarray(cos_full[:, own_idx]),
            "sin_q": np.ascontiguousarray(sin_full[:, own_idx]),
            "masks": masks, "ident": ident, "pmat": pmat,
        })
    return in_maps


# version 07
# speedup vs baseline: 1.0701x; 1.0701x over previous
import math
from collections import defaultdict

import numpy as np

import concourse.bass as bass
import concourse.mybir as mybir
from concourse.bass_utils import run_bass_kernel_spmd

F32 = mybir.dt.float32
BF16 = mybir.dt.bfloat16
AF = mybir.ActivationFunctionType
ALU = mybir.AluOpType
AX = mybir.AxisListType

B, L, D = 4, 2048, 2048
NH, HD = 16, 128
DF = 8192
EPS = 1e-5
SCALE = HD ** -0.5
NT = L // 128
NOWN = 8
KC = [1, 1, 2, 2, 3, 3, 4, 4]
NEG = -1.0e30


class Op:
    __slots__ = ("eng", "fn", "deps", "inc", "idx", "after", "need", "qsnap")

    def __init__(self, eng, fn, deps, inc, idx):
        self.eng, self.fn, self.deps, self.inc, self.idx = eng, fn, deps, inc, idx
        self.after = 0
        self.need = inc == 16


class Res:
    __slots__ = ("w", "r")

    def __init__(self):
        self.w = None
        self.r = []


ENGS = ("sync", "gpsimd", "tensor", "vector", "scalar")


class Sched:
    def __init__(self):
        self.ops = []
        self.fence_deps = []
        self.R = defaultdict(Res)
        self.enabled = True

    def add(self, eng, fn, reads=(), writes=(), extra=(), dma=False):
        if not self.enabled:
            return None
        R = self.R
        deps = list(self.fence_deps)
        for name in reads:
            res = R[name]
            if res.w is not None:
                deps.append(res.w)
        for name in writes:
            res = R[name]
            if res.w is not None:
                deps.append(res.w)
            deps.extend(res.r)
        deps.extend(d for d in extra if d is not None)
        best = {}
        for d in deps:
            k = d.eng
            if k not in best or d.idx > best[k].idx:
                best[k] = d
        deps = list(best.values())
        op = Op(eng, fn, deps, 16 if dma else 1, len(self.ops))
        for d in deps:
            assert d.idx < op.idx
        self.ops.append(op)
        for name in reads:
            R[name].r.append(op)
        for name in writes:
            R[name].w = op
            R[name].r = []
        return op

    def fence(self):
        last = {}
        for op in self.ops:
            last[op.eng] = op
        self.fence_deps = list(last.values())

    def finalize(self):
        for op in self.ops:
            for d in op.deps:
                if d.eng != op.eng or d.inc == 16:
                    d.need = True
        cum = defaultdict(int)
        dma_cum = defaultdict(int)
        for op in self.ops:
            op.qsnap = dict(dma_cum)
            if op.need:
                cum[op.eng] += op.inc
                if op.inc == 16:
                    dma_cum[op.eng] = cum[op.eng]
            op.after = cum[op.eng]

    def emit_for(self, engname, eng, sems, sep_ap=None):
        hw = defaultdict(int)
        prev = None
        nsep = 0
        for op in self.ops:
            if op.eng != engname:
                continue
            for d in op.deps:
                if d.eng == engname and d.inc != 16:
                    continue
                val = op.qsnap.get(d.eng, 0) if d.inc == 16 else d.after
                assert val >= d.after or d.inc != 16
                if val > hw[d.eng]:
                    eng.wait_ge(sems[d.eng], val)
                    hw[d.eng] = val
            if engname == "gpsimd" and op.inc == 16:
                val = op.qsnap.get("gpsimd", 0)
                if val > hw["gpsimd"]:
                    eng.wait_ge(sems["gpsimd"], val)
                    hw["gpsimd"] = val
            if (sep_ap is not None and prev is not None
                    and any(d is prev for d in op.deps)):
                if engname == "scalar":
                    eng.mul(sep_ap, sep_ap, 1.0)
                else:
                    eng.tensor_scalar_mul(sep_ap, sep_ap, 1.0)
                nsep += 1
            if op.fn is not None:
                ins = op.fn(eng)
                if op.need:
                    ins.then_inc(sems[op.eng], op.inc)
                prev = op
        self.nsep = getattr(self, "nsep", 0) + nsep


def kc_of(g):
    return (g + 4) // 4


def _load_table_set(nc, want):
    from concourse.hw_specs import get_activation_tables
    for idx, funcs in enumerate(get_activation_tables(nc.m.arch).values()):
        if all(f in funcs for f in want):
            def emit(e, idx=idx):
                return e.add_instruction(mybir.InstLoadActFuncSet(
                    name=nc.get_next_instruction_name(),
                    act_func_set_id=idx, ins=[], outs=[]))
            return emit
    raise ValueError(f"no table set contains {want}")


def build(nphases=99, p2lvl=99, dump_sp=False, kinds="KVQG"):
    nc = bass.Bass("TRN2", target_bir_lowering=False,
                   detect_race_conditions=False)

    def din(name, shape):
        return nc.dram_tensor(name, shape, F32, kind="ExternalInput").ap()

    x_kv = din("x_kv", [L, D])
    x_own = din("x_own", [NOWN * 128, D])
    wqT = din("wqT", [8, 128, 16, 256])
    wkT = din("wkT", [8, 128, 16, 256])
    wvT = din("wvT", [8, 128, 16, 256])
    wgT = din("wgT", [8, 128, 16, 256])
    woT = din("woT", [8, 128, 16, 256])
    gateT = din("gateT", [64, 128, 16, 128])
    upT = din("upT", [64, 128, 16, 128])
    downT = din("downT", [2, 4, 8, 128, 2048])
    wmod = din("wmod", [D, 6 * D])
    c_fm = din("c_fm", [128, 16])
    nwrep_in = din("normw_rep", [128, 128])
    cos_k = din("cos_k", [128, L])
    sin_k = din("sin_k", [128, L])
    cos_q = din("cos_q", [128, NOWN * 128])
    sin_q = din("sin_q", [128, NOWN * 128])
    masks_in = din("masks", [128, NOWN, 512])
    id_in = din("ident", [128, 128])
    pmat_in = din("pmat", [128, 128])

    y_own = nc.dram_tensor("y_own", [NOWN * 128, D], F32,
                           kind="ExternalOutput").ap()
    spkind = "ExternalOutput" if dump_sp else "Internal"
    Vsp = nc.dram_tensor("Vsp", [NH, L, HD], BF16, kind=spkind).ap()
    Gsp = nc.dram_tensor("Gsp", [NH, NOWN * 128, HD], BF16, kind=spkind).ap()
    if dump_sp:
        dbg_x2 = nc.dram_tensor("dbg_x2", [NOWN * 128, D], F32,
                                kind="ExternalOutput").ap()
        dbg_h2 = nc.dram_tensor("dbg_h2", [128, 16, 1024], BF16,
                                kind="ExternalOutput").ap()
        dbg_gat = nc.dram_tensor("dbg_gat", [128, 16, 1024], BF16,
                                 kind="ExternalOutput").ap()
        dbg_gated = nc.dram_tensor("dbg_gated", [128, 32, 1024], BF16,
                                   kind="ExternalOutput").ap()
        dbg_h = nc.dram_tensor("dbg_h", [128, 16, 2048], BF16,
                               kind="ExternalOutput").ap()
        dbg_st = nc.dram_tensor("dbg_st", [128, 6, 16], F32,
                                kind="ExternalOutput").ap()
        dbg_K = nc.dram_tensor("dbg_K", [128, 16, 2048], BF16,
                               kind="ExternalOutput").ap()
        dbg_Q = nc.dram_tensor("dbg_Q", [128, 16, 1024], BF16,
                               kind="ExternalOutput").ap()
    rows_d = nc.dram_tensor("rows_d", [2, D], F32).ap()

    S = Sched()

    from contextlib import ExitStack
    with ExitStack() as ctx:
        def sb(name, shape, dt):
            return ctx.enter_context(nc.sbuf_tensor(name, shape, dt))

        arA = sb("arA", [128, 16, 2048], BF16)
        arB = sb("arB", [128, 16, 2048], BF16)
        arC = sb("arC", [128, 16, 1024], BF16)
        arD = sb("arD", [128, 16, 1024], BF16)
        w_sb = sb("w_sb", [128, 16, 256], BF16)
        bc_sb = sb("bc_sb", [128, 2, 512], F32)
        nwrep = sb("nwrep", [128, 128], F32)
        sm = sb("sm", [128, 128], F32)
        stats = sb("stats", [128, 6, 16], F32)
        stat2 = sb("stat2", [128, 3, 16], F32)
        modb = sb("modb", [128, 2, 96], F32)
        modc = sb("modc", [128, 2, 16], BF16)
        id_bf = sb("id_bf", [128, 128], BF16)
        pmat = sb("pmat_s", [128, 128], BF16)
        c_sb = sb("c_sb", [128, 16], F32)
        bias_sb = sb("bias_sb", [128, 1], F32)
        sepv = sb("sepv", [128, 1], F32)
        seps = sb("seps", [128, 1], F32)
        ps0 = ctx.enter_context(nc.psum_tensor("ps0", [128, 512], F32))
        ps1 = ctx.enter_context(nc.psum_tensor("ps1", [128, 512], F32))
        ps2 = ctx.enter_context(nc.psum_tensor("ps2", [128, 512], F32))
        ps3 = ctx.enter_context(nc.psum_tensor("ps3", [128, 512], F32))
        ps4 = ctx.enter_context(nc.psum_tensor("ps4", [128, 512], F32))
        ps5 = ctx.enter_context(nc.psum_tensor("ps5", [128, 512], F32))
        ps6 = ctx.enter_context(nc.psum_tensor("ps6", [128, 512], F32))
        ptr = ctx.enter_context(nc.psum_tensor("ptr", [128, 1024], BF16))
        sem_sync = ctx.enter_context(nc.semaphore("sem_sync"))
        sem_gps = ctx.enter_context(nc.semaphore("sem_gps"))
        sem_ten = ctx.enter_context(nc.semaphore("sem_ten"))
        sem_vec = ctx.enter_context(nc.semaphore("sem_vec"))
        sem_sca = ctx.enter_context(nc.semaphore("sem_sca"))
        sems = {"sync": sem_sync, "gpsimd": sem_gps, "tensor": sem_ten,
                "vector": sem_vec, "scalar": sem_sca}
        psb = [ps0, ps1, ps2, ps3, ps4, ps5, ps6]
        ptr_f32 = ptr[:].bitcast(F32)
        ps1_bf = ps1[:].bitcast(BF16)

        arA_fl = arA[:].rearrange("p a b -> p (a b)")
        arA_f32 = arA_fl.bitcast(F32)
        h_fm = arA
        wmod_f32 = arA_f32[:, 0:4096]
        wmod_f32b = arA_f32[:, 4096:8192]
        scores_sb = arA_f32[:, 0:2048]
        e_sb = arA_f32[:, 2048:4096]
        sg_sb = arA_f32[:, 4096:6144]
        av_sb = arA_f32[:, 6144:8192].rearrange("p (a b) -> p a b", b=128)
        p_sb = arA_fl[:, 16384:18432]
        p_sbB = arA_fl[:, 30208:32256]
        pT_sb = arA_fl[:, 18432:18944]
        pT_sb2 = arA_fl[:, 27648:28160]
        gtm_sb = arA_fl[:, 18944:19072]
        Gt_fl = arA_fl[:, 19456:21504]
        Gt = Gt_fl.rearrange("p (a b) -> p a b", b=128)
        v_fl = arA_fl[:, 21504:23552]
        v_sb = v_fl.rearrange("p (a b) -> p a b", b=128)
        v_fl2 = arA_fl[:, 28160:30208]
        v_sb2 = v_fl2.rearrange("p (a b) -> p a b", b=128)
        mask_fl = arA_fl[:, 23552:27648]
        mask_sb = mask_fl.rearrange("p (a b) -> p a b", b=512)
        x3_sb = arA_f32[:, 0:2048]
        gated = arA_fl.rearrange("p (a b) -> p a b", b=1024)

        K_fm = arB
        x2 = arB[:].rearrange("p a b -> p (a b)").bitcast(F32).rearrange(
            "p (a b) -> p a b", b=2048)
        Q_own = arC
        h2_fm = arC

        arD_fl = arD[:].rearrange("p a b -> p (a b)")
        arD_f32 = arD_fl.bitcast(F32)
        x_sb = arD_f32[:, 0:2048]
        x_sb2 = arD_f32[:, 5120:7168]
        trashD = arD_f32[:, 2048:4096]
        xn_sb = arD_fl[:, 8192:10240]
        cosk_sb = arD_fl[:, 0:2048]
        sink_sb = arD_fl[:, 2048:4096]
        cosq_sb = arD_fl[:, 4096:5120]
        sinq_sb = arD_fl[:, 5120:6144]
        a_rp = arD_f32[:, 3072:3584]
        b_rp = arD_f32[:, 3584:4096]
        kraw = arD_fl[:, 8192:8704]
        w2_sb = arD_fl[:, 8704:12800].rearrange("p (a b) -> p a b", b=256)
        gat_fm = arD
        silu_t = arD_f32[:, 2048:2560]
        w5B = arD_fl[:, 8192:12288]
        silu_t2 = arD_f32[:, 2560:3072]

        w_fl = w_sb[:].rearrange("p a b -> p (a b)")
        vstage = bc_sb[:].rearrange("p a b -> p (a b)").bitcast(BF16)

        hK = "hK"

        def dma(eng, out, in_, reads=(), writes=(), extra=()):
            return S.add(eng, lambda e, o=out, i=in_: e.dma_start(out=o, in_=i),
                         reads=reads, writes=writes, extra=extra, dma=True)

        def vop(fn, reads=(), writes=(), eng="vector", extra=()):
            return S.add(eng, fn, reads=reads, writes=writes, extra=extra)

        S.add("vector", lambda e: e.memset(sepv[:], 0.0))
        S.add("scalar", lambda e: e.memzero(seps[:]))
        dma("sync", c_sb[:], c_fm, writes=["c"])
        dma("sync", nwrep[:], nwrep_in, writes=["nw"])
        dma("gpsimd", id_bf[:], id_in, writes=["id"])
        dma("gpsimd", pmat[:], pmat_in, writes=["pm"])
        S.add("scalar", _load_table_set(nc, (AF.Exp, AF.Ln, AF.Square)),
              writes=["tab"])
        S.add("scalar", lambda e: e.activation(stats[:, 0, :], c_sb[:], AF.Exp,
                                               scale=-1.0),
              reads=["c", "tab"], writes=["st"])
        vop(lambda e: e.tensor_scalar_add(stats[:, 0, :], stats[:, 0, :], 1.0),
            reads=["st"], writes=["st"])
        vop(lambda e: e.reciprocal(stats[:, 1, :], stats[:, 0, :]),
            reads=["st"], writes=["st"])
        vop(lambda e: e.tensor_mul(c_sb[:], c_sb[:], stats[:, 1, :]),
            reads=["st", "c"], writes=["sc"])

        S.add("vector", lambda e: e.memset(bc_sb[0:1, 1, 0:1], 1.0),
              writes=["bcid"])
        rowtmp = bc_sb[0:1, 0, :]
        allb = [ps0, ps1, ps2, ps3, ps4, ps5, ps6, ptr_f32]
        allr = ["b0", "b1", "b2", "b3", "b4", "b5", "b6", "b7"]
        for third in range(3):
            for k in range(16):
                slab = wmod_f32 if k % 2 == 0 else wmod_f32b
                sres = "wmod%d" % (k % 2)
                dma("sync", slab,
                    wmod[k * 128:(k + 1) * 128,
                         third * 4096:(third + 1) * 4096],
                    writes=[sres])
                for jj in range(8):
                    S.add("tensor",
                          lambda e, k=k, jj=jj, slab=slab: e.matmul(
                              allb[jj][0:1, 0:512], c_sb[:, k:k + 1],
                              slab[:, jj * 512:(jj + 1) * 512],
                              start=(k == 0), stop=(k == 15),
                              skip_group_check=True),
                          reads=[sres, "sc"], writes=[allr[jj]])
            for jj in range(8):
                j = third * 8 + jj
                vop(lambda e, jj=jj: e.tensor_copy(rowtmp, allb[jj][0:1, 0:512]),
                    reads=[allr[jj]], writes=["rowt"])
                for q in range(4):
                    S.add("tensor",
                          lambda e, q=q: e.transpose(
                              ps0[0:128, 0:1],
                              rowtmp[0:1, q * 128:(q + 1) * 128],
                              bc_sb[0:1, 1, 0:1]),
                          reads=["rowt", "bcid"], writes=["b0"])
                    vop(lambda e, j=j, q=q: e.tensor_copy(
                        modb[:, 0, j * 4 + q:j * 4 + q + 1],
                        ps0[0:128, 0:1]),
                        reads=["b0"], writes=["mraw"])

        vop(lambda e: e.tensor_copy(modb[:, 1, 0:16], modb[:, 0, 0:16]),
            reads=["mraw"], writes=["mfm"])
        vop(lambda e: e.tensor_scalar_add(modb[:, 1, 16:32], modb[:, 0, 16:32], 1.0),
            reads=["mraw"], writes=["mfm"])
        vop(lambda e: e.tensor_copy(modb[:, 1, 48:64], modb[:, 0, 48:64]),
            reads=["mraw"], writes=["mfm"])
        vop(lambda e: e.tensor_scalar_add(modb[:, 1, 64:80], modb[:, 0, 64:80], 1.0),
            reads=["mraw"], writes=["mfm"])
        for seg in (32, 80):
            vop(lambda e, s=seg: e.tensor_scalar_max(stats[:, 0, :],
                                                     modb[:, 0, s:s + 16], 0.25),
                reads=["mraw"], writes=["st"])
            vop(lambda e, s=seg: e.tensor_scalar_min(stats[:, 1, :],
                                                     modb[:, 0, s:s + 16], -0.25),
                reads=["mraw"], writes=["st"])
            vop(lambda e, s=seg: e.tensor_scalar(stats[:, 2, :],
                                                 modb[:, 0, s:s + 16], 0.0, None,
                                                 ALU.is_ge),
                reads=["mraw"], writes=["st"])
            vop(lambda e: e.tensor_sub(stats[:, 3, :], stats[:, 0, :],
                                       stats[:, 1, :]),
                reads=["st"], writes=["st"])
            vop(lambda e: e.tensor_mul(stats[:, 3, :], stats[:, 3, :],
                                       stats[:, 2, :]),
                reads=["st"], writes=["st"])
            vop(lambda e, s=seg: e.tensor_add(modb[:, 1, s:s + 16],
                                              stats[:, 1, :], stats[:, 3, :]),
                reads=["st"], writes=["mfm"])
        for i, seg in enumerate((32, 80)):
            vop(lambda e, i=i, s=seg: e.tensor_copy(modc[:, i, :],
                                                    modb[:, 1, s:s + 16]),
                reads=["mfm"], writes=["modc"])
            S.add("tensor",
                  lambda e, i=i: e.transpose(ptr[0:16, 0:128], modc[:, i, :],
                                             id_bf[:]),
                  reads=["modc", "id"], writes=["b7"])
            vop(lambda e, i=i: e.tensor_copy(bc_sb[0:16, i, 0:128],
                                             ptr[0:16, 0:128]),
                reads=["b7"], writes=["garow"])
        dma("sync", rows_d[0, :], bc_sb[0:16, 0, 0:128], reads=["garow"], writes=["rows"])
        dma("sync", rows_d[1, :], bc_sb[0:16, 1, 0:128], reads=["garow"], writes=["rows"])

        def modcol(base, k):
            return modb[:, 1, base + k:base + k + 1]

        S.fence()

        def emit_ln(x_tile, x_res, bb, opc_base, sft_base, out_t, out_col, out_res):
            S.add("scalar",
                  lambda e, x_tile=x_tile, bb=bb: e.activation(
                      trashD, x_tile, AF.Square,
                      accum_out=stats[:, 1, bb:bb + 1]),
                  reads=[x_res], writes=["trD", "st"])
            vop(lambda e, x_tile=x_tile, bb=bb: e.reduce_sum(
                stats[:, 0, bb:bb + 1], x_tile, AX.X),
                reads=[x_res], writes=["st"])
            vop(lambda e, bb=bb: e.tensor_scalar_mul(
                stats[:, 2, bb:bb + 1], stats[:, 0, bb:bb + 1], 1.0 / D),
                reads=["st"], writes=["st"])
            vop(lambda e, bb=bb: e.tensor_mul(
                stats[:, 5, bb:bb + 1], stats[:, 2, bb:bb + 1],
                stats[:, 2, bb:bb + 1]),
                reads=["st"], writes=["st"])
            vop(lambda e, bb=bb: e.tensor_scalar(
                stats[:, 4, bb:bb + 1], stats[:, 1, bb:bb + 1], 1.0 / D, EPS,
                ALU.mult, ALU.add),
                reads=["st"], writes=["st"])
            vop(lambda e, bb=bb: e.tensor_sub(
                stats[:, 4, bb:bb + 1], stats[:, 4, bb:bb + 1],
                stats[:, 5, bb:bb + 1]),
                reads=["st"], writes=["st"])
            S.add("scalar",
                  lambda e, bb=bb: e.activation(stats[:, 5, bb:bb + 1],
                                                stats[:, 4, bb:bb + 1], AF.Ln),
                  reads=["st"], writes=["st"])
            S.add("scalar",
                  lambda e, bb=bb: e.activation(stats[:, 3, bb:bb + 1],
                                                stats[:, 5, bb:bb + 1], AF.Exp,
                                                scale=-0.5),
                  reads=["st"], writes=["st"])
            vop(lambda e, x_tile=x_tile, bb=bb: e.tensor_scalar(
                xn_sb, x_tile, stats[:, 2, bb:bb + 1], stats[:, 3, bb:bb + 1],
                ALU.subtract, ALU.mult),
                reads=[x_res, "st"], writes=["xn"])
            for half in range(2):
                for kk in range(8):
                    k = half * 8 + kk
                    S.add("tensor",
                          lambda e, k=k, kk=kk: e.transpose(
                              ptr[:, kk * 128:(kk + 1) * 128],
                              xn_sb[:, k * 128:(k + 1) * 128], id_bf[:]),
                          reads=["xn", "id"], writes=["b7"])
                for kk in range(8):
                    k = half * 8 + kk
                    vop(lambda e, k=k, kk=kk, out_t=out_t, out_col=out_col,
                        ob=opc_base, sb=sft_base: e.tensor_scalar(
                            out_t[:, k, out_col:out_col + 128],
                            ptr[:, kk * 128:(kk + 1) * 128],
                            modcol(ob, k), modcol(sb, k), ALU.mult, ALU.add),
                        reads=["b7", "mfm"], writes=[out_res])

        S.enabled = nphases >= 2
        for blk in range(NT):
            xa = x_sb if blk % 2 == 0 else x_sb2
            xr = "xs%d" % (blk % 2)
            dma("sync", xa, x_kv[blk * 128:(blk + 1) * 128, :], writes=[xr])
            emit_ln(xa, xr, blk % 16, 16, 0, h_fm, blk * 128, hK)

        S.fence()
        if dump_sp:
            dma("sync", dbg_h, arA[:], reads=[hK], writes=["dbghh"])
            dma("sync", dbg_st, stats[:], reads=["st"], writes=["dbgst"])

        S.enabled = nphases >= 3
        dma("gpsimd", cosk_sb, cos_k, writes=["tabs"])
        dma("gpsimd", sink_sb, sin_k, writes=["tabs"])
        dma("gpsimd", cosq_sb, cos_q, writes=["tabs"])
        dma("gpsimd", sinq_sb, sin_q, writes=["tabs"])

        hAP = h_fm[:]
        pstep_h = hAP.ap[0][0]

        def own_rhs(k, oc):
            off = k * 2048 + 1024 * oc
            return bass.AP(arA, off, [[pstep_h, 128], [256, 4], [1, 128]])

        def rope_evac(pbank, pres, cos_t, sin_t, col0, dst, wname):
            vop(lambda e, pbank=pbank: e.tensor_copy(kraw, pbank[:, 0:512]),
                reads=[pres], writes=["kraw"])
            vop(lambda e, cos_t=cos_t, col0=col0, pbank=pbank: e.tensor_mul(
                a_rp, pbank[:, 0:512], cos_t[:, col0:col0 + 512]),
                reads=[pres, "tabs"], writes=["arp"])
            S.add("tensor",
                  lambda e: e.matmul(ps2[:, 0:512], pmat[:], kraw,
                                     start=True, stop=True),
                  reads=["kraw", "pm"], writes=["b2"])
            vop(lambda e, sin_t=sin_t, col0=col0: e.tensor_mul(
                b_rp, ps2[:, 0:512], sin_t[:, col0:col0 + 512]),
                reads=["b2", "tabs"], writes=["brp"])
            vop(lambda e, dst=dst: e.tensor_add(dst, a_rp, b_rp),
                reads=["arp", "brp"], writes=[wname])

        wcnt = 0
        for wT, kind in ((wkT, "K"), (wvT, "V"), (wqT, "Q"), (wgT, "G")):
            if kind not in kinds:
                continue
            for fc in range(8):
                wcnt += 1
                wb = w_sb[:] if wcnt % 2 == 0 else w2_sb
                wres = "w%d" % (wcnt % 2)
                dma("gpsimd", wb, wT[fc], writes=[wres])
                if kind in ("K", "Q"):
                    for hl in range(2):
                        h = fc * 2 + hl
                        ncols = 4 if kind == "K" else 2
                        for tc in range(ncols):
                            par = (hl * 4 + tc) % 2
                            pbank = ps0 if par == 0 else ps4
                            pres = "b0" if par == 0 else "b4"
                            for k in range(16):
                                rhs = (h_fm[:, k, tc * 512:(tc + 1) * 512]
                                       if kind == "K" else own_rhs(k, tc))
                                S.add("tensor",
                                      lambda e, k=k, hl=hl, rhs=rhs,
                                      pbank=pbank, wb=wb: e.matmul(
                                          pbank[:, 0:512],
                                          wb[:, k, hl * 128:(hl + 1) * 128],
                                          rhs, start=(k == 0), stop=(k == 15)),
                                      reads=[wres, hK], writes=[pres])
                            if kind == "K":
                                rope_evac(pbank, pres, cosk_sb, sink_sb,
                                          tc * 512,
                                          K_fm[:, h, tc * 512:(tc + 1) * 512],
                                          "Kf")
                            else:
                                rope_evac(pbank, pres, cosq_sb, sinq_sb,
                                          tc * 512,
                                          Q_own[:, h, tc * 512:(tc + 1) * 512],
                                          "Qf")
                else:
                    nblk = 16 if kind == "V" else NOWN
                    spname = "Vsp" if kind == "V" else "Gsp"
                    dstT = Vsp if kind == "V" else Gsp
                    seqlen = L if kind == "V" else NOWN * 128
                    for tb in range(nblk):
                        lo = tb * 128 if kind == "V" else 256 * tb
                        pbank = ps0 if tb % 2 == 0 else ps4
                        pres = "b0" if tb % 2 == 0 else "b4"
                        for k in range(16):
                            S.add("tensor",
                                  lambda e, k=k, lo=lo, pbank=pbank,
                                  wb=wb: e.matmul(
                                      pbank[:, 0:256],
                                      bass.AP(arA, k * 2048 + lo,
                                              [[pstep_h, 128], [1, 128]]),
                                      wb[:, k, :], start=(k == 0),
                                      stop=(k == 15)),
                                  reads=[wres, hK], writes=[pres])
                        vop(lambda e, tb=tb, pbank=pbank: e.tensor_copy(
                            vstage[:, (tb % 4) * 512:(tb % 4) * 512 + 256],
                            pbank[:, 0:256]),
                            reads=[pres], writes=["vst"])
                        if tb % 4 == 3 or tb == nblk - 1:
                            nb = tb % 4 + 1
                            tb0 = tb - nb + 1
                            for hh in range(2):
                                dst_ap = bass.AP(
                                    dstT.tensor,
                                    (2 * fc + hh) * seqlen * 128 + tb0 * 128 * 128,
                                    [[128, 128], [128 * 128, nb], [1, 128]])
                                src_ap = bass.AP(
                                    vstage.tensor,
                                    vstage.offset + hh * 128,
                                    [list(vstage.ap[0]), [512, nb], [1, 128]])
                                dma("sync", dst_ap, src_ap, reads=["vst"],
                                    writes=[spname])

        S.fence()
        if dump_sp:
            dma("sync", dbg_K, arB[:], reads=["Kf"], writes=["dbgk"])
            dma("sync", dbg_Q, arC[:], reads=["Qf"], writes=["dbgq"])

        S.enabled = nphases >= 4
        dma("gpsimd", mask_fl, masks_in.rearrange("p a c -> p (a c)"),
            writes=["mask"])
        for i in range(NOWN):
            kc = KC[i]
            dma("sync", Gt_fl,
                bass.AP(Gsp.tensor, i * 128 * 128,
                        [[128, 128], [NOWN * 128 * 128, 16], [1, 128]]),
                reads=["Gsp"], writes=["Gt"])
            S.add("scalar", lambda e: e.activation(e_sb, Gt_fl, AF.Exp,
                                                   scale=-1.0),
                  reads=["Gt", "tab"], writes=["es"])
            vop(lambda e: e.tensor_scalar_add(e_sb, e_sb, 1.0),
                reads=["es"], writes=["es"])
            S.add("scalar", lambda e: e.activation(e_sb, e_sb, AF.Ln),
                  reads=["es"], writes=["es"])
            S.add("scalar", lambda e: e.activation(e_sb, e_sb, AF.Exp,
                                                   scale=-1.0),
                  reads=["es"], writes=["es"])
            vop(lambda e: e.tensor_mul(sg_sb, Gt_fl, e_sb),
                reads=["es", "Gt"], writes=["sg"])
            def stage_A(h):
                if p2lvl < 2:
                    return
                vfl = v_fl if h % 2 == 0 else v_fl2
                vres = "vv%d" % (h % 2)
                dma("sync", vfl[:, 0:kc * 512],
                    bass.AP(Vsp.tensor, h * L * 128,
                            [[128, 128], [128 * 128, kc * 4], [1, 128]]),
                    reads=["Vsp"], writes=[vres])
                pcur = p_sb if h % 2 == 0 else p_sbB
                pres0 = "pp" if h % 2 == 0 else "ppB"
                sb_l = [ps0, ps4, ps5, ps6]
                sr_l = ["b0", "b4", "b5", "b6"]
                for j in range(kc):
                    S.add("tensor",
                          lambda e, h=h, i=i, j=j: e.matmul(
                              sb_l[j][:, 0:512],
                              Q_own[:, h, i * 128:(i + 1) * 128],
                              K_fm[:, h, j * 512:(j + 1) * 512],
                              start=True, stop=True),
                          reads=["Qf", "Kf"], writes=[sr_l[j]])
                vop(lambda e, kc=kc, i=i: e.tensor_add(
                    sb_l[kc - 1][:, 0:512], sb_l[kc - 1][:, 0:512],
                    mask_sb[:, i, :]),
                    reads=[sr_l[kc - 1], "mask"], writes=[sr_l[kc - 1]])
                for j in range(kc):
                    dst = bias_sb[:] if j == 0 else stats[:, 5, 15:16]
                    vop(lambda e, j=j, dst=dst: e.reduce_max(
                        dst, sb_l[j][:, 0:512], AX.X, negate=True),
                        reads=[sr_l[j]], writes=["bi"])
                    if j > 0:
                        vop(lambda e: e.tensor_tensor(
                            bias_sb[:], bias_sb[:], stats[:, 5, 15:16],
                            ALU.min),
                            reads=["bi"], writes=["bi"])
                vop(lambda e: e.tensor_scalar_mul(bias_sb[:], bias_sb[:], SCALE),
                    reads=["bi"], writes=["bi"])
                for j in range(kc):
                    acc = (stat2[:, 0, h:h + 1] if kc == 1
                           else stats[:, j, 15:16])
                    S.add("scalar",
                          lambda e, j=j, pcur=pcur, acc=acc: e.activation(
                              pcur[:, j * 512:(j + 1) * 512],
                              sb_l[j][:, 0:512], AF.Exp, bias=bias_sb[:],
                              scale=SCALE, accum_out=acc),
                          reads=[sr_l[j], "bi", "tab"],
                          writes=[pres0, "l2" if kc == 1 else "lt%d" % j])
                if kc > 1:
                    vop(lambda e, h=h: e.tensor_add(stat2[:, 0, h:h + 1],
                                                    stats[:, 0, 15:16],
                                                    stats[:, 1, 15:16]),
                        reads=["lt0", "lt1"], writes=["l2"])
                    for j in range(2, kc):
                        vop(lambda e, j=j, h=h: e.tensor_add(
                            stat2[:, 0, h:h + 1], stat2[:, 0, h:h + 1],
                            stats[:, j, 15:16]),
                            reads=["lt%d" % j], writes=["l2"])

            def stage_B(h):
                if p2lvl < 5:
                    return
                pcur = p_sb if h % 2 == 0 else p_sbB
                pres0 = "pp" if h % 2 == 0 else "ppB"
                vsb = v_sb if h % 2 == 0 else v_sb2
                vres = "vv%d" % (h % 2)
                for j in range(kc):
                    tb_ap = ptr if j % 2 == 0 else ps1_bf
                    tb_res = "b7" if j % 2 == 0 else "b1"
                    pT_dst = pT_sb if j % 2 == 0 else pT_sb2
                    pT_res = "pT" if j % 2 == 0 else "pT2"
                    for s4 in range(4):
                        S.add("tensor",
                              lambda e, j=j, s4=s4, tb_ap=tb_ap,
                              pcur=pcur: e.transpose(
                                  tb_ap[:, s4 * 128:(s4 + 1) * 128],
                                  pcur[:, j * 512 + s4 * 128:
                                       j * 512 + (s4 + 1) * 128], id_bf[:]),
                              reads=[pres0, "id"], writes=[tb_res])
                    S.add("scalar",
                          lambda e, tb_ap=tb_ap, pT_dst=pT_dst: e.copy(
                              pT_dst, tb_ap[:, 0:512]),
                          reads=[tb_res], writes=[pT_res])
                    for s4 in range(4):
                        S.add("tensor",
                              lambda e, j=j, s4=s4, kc=kc, pT_dst=pT_dst,
                              vsb=vsb: e.matmul(
                                  ps2[:, 0:128],
                                  pT_dst[:, s4 * 128:(s4 + 1) * 128],
                                  vsb[:, j * 4 + s4, :],
                                  start=(j == 0 and s4 == 0),
                                  stop=(j == kc - 1 and s4 == 3)),
                              reads=[pT_res, vres], writes=["b2"])
                if p2lvl < 6:
                    return
                sq_op = S.add("scalar",
                              lambda e, h=h: e.activation(
                                  bc_sb[:, 1, 0:128], ps2[:, 0:128], AF.Square,
                                  accum_out=stat2[:, 1, h:h + 1]),
                              reads=["b2"], writes=["sm1", "sq2"])
                vop(lambda e, h=h: e.tensor_copy(av_sb[:, h, :], ps2[:, 0:128]),
                    reads=["b2"], writes=["av"], extra=[sq_op])

            stage_A(0)
            for h in range(1, NH):
                stage_A(h)
                stage_B(h - 1)
            stage_B(NH - 1)
            if p2lvl < 7:
                continue
            vop(lambda e: e.tensor_mul(stats[:, 0, :], stat2[:, 0, :],
                                       stat2[:, 0, :]),
                reads=["l2"], writes=["st"])
            vop(lambda e: e.tensor_scalar_mul(stats[:, 1, :], stat2[:, 1, :],
                                              1.0 / HD),
                reads=["sq2"], writes=["st"])
            vop(lambda e: e.scalar_tensor_tensor(stats[:, 1, :], stats[:, 0, :],
                                                 EPS, stats[:, 1, :],
                                                 ALU.mult, ALU.add),
                reads=["st"], writes=["st"])
            S.add("scalar", lambda e: e.activation(stats[:, 0, :],
                                                   stats[:, 1, :], AF.Ln),
                  reads=["st"], writes=["st"])
            S.add("scalar", lambda e: e.activation(stat2[:, 2, :],
                                                   stats[:, 0, :], AF.Exp,
                                                   scale=-0.5),
                  reads=["st"], writes=["al"])
            for hq in range(4):
                for hl in range(4):
                    h = hq * 4 + hl
                    vop(lambda e, h=h: e.tensor_mul(
                        sm[:, :], sg_sb[:, h * 128:(h + 1) * 128], nwrep[:]),
                        reads=["sg", "nw"], writes=["sm0"])
                    vop(lambda e, h=h: e.scalar_tensor_tensor(
                        gtm_sb, av_sb[:, h, :], stat2[:, 2, h:h + 1],
                        sm[:, :], ALU.mult, ALU.mult),
                        reads=["av", "al", "sm0"], writes=["gtm"])
                    S.add("tensor",
                          lambda e, hl=hl: e.transpose(
                              ptr[:, hl * 128:(hl + 1) * 128], gtm_sb,
                              id_bf[:]),
                          reads=["gtm", "id"], writes=["b7"])
                vop(lambda e, hq=hq, i=i: e.tensor_copy(
                    bass.AP(arD, hq * 4 * 1024 + i * 128,
                            [[16 * 1024, 128], [1024, 4], [1, 128]]),
                    ptr[:, 0:512].rearrange("p (a b) -> p a b", b=128)),
                    reads=["b7"], writes=["gat"])

        S.fence()
        if dump_sp:
            dma("sync", dbg_gat, arD[:], reads=["gat"], writes=["dbgg"])

        S.enabled = nphases >= 5
        for dc in range(8):
            dma("sync", bc_sb[:, 0, 0:256],
                bass.AP(rows_d.tensor, dc * 256, [[0, 128], [1, 256]]),
                reads=["rows"], writes=["bc0"])
            dma("gpsimd", w_sb[:], woT[dc], writes=["w"])
            for ot in range(NOWN):
                pbank = ps0 if ot % 2 == 0 else ps4
                pres = "b0" if ot % 2 == 0 else "b4"
                for k in range(16):
                    S.add("tensor",
                          lambda e, k=k, ot=ot, pbank=pbank: e.matmul(
                              pbank[:, 0:256],
                              bass.AP(arD, k * 1024 + ot * 128,
                                      [[16 * 1024, 128], [1, 128]]),
                              w_sb[:, k, :], start=(k == 0), stop=(k == 15)),
                          reads=["w", "gat"], writes=[pres])
                vop(lambda e, ot=ot, dc=dc, pbank=pbank: e.tensor_mul(
                    x2[:, ot, dc * 256:(dc + 1) * 256], pbank[:, 0:256],
                    bc_sb[:, 0, 0:256]),
                    reads=[pres, "bc0"], writes=["x2"])
        for ot in range(NOWN):
            dma("sync", x3_sb, x_own[ot * 128:(ot + 1) * 128, :], writes=["x3"])
            vop(lambda e, ot=ot: e.tensor_add(x2[:, ot, :], x2[:, ot, :], x3_sb),
                reads=["x3"], writes=["x2"])
        S.fence()
        if dump_sp:
            for ot in range(NOWN):
                dma("sync", dbg_x2[ot * 128:(ot + 1) * 128, :], x2[:, ot, :],
                    reads=["x2"], writes=["dbgx"])

        S.enabled = nphases >= 6
        for ot in range(NOWN):
            emit_ln(x2[:, ot, :], "x2", ot, 64, 48, h2_fm, ot * 128, "h2")

        S.fence()

        if dump_sp:
            dma("sync", dbg_h2, arC[:], reads=["h2"], writes=["dbgh"])

        S.enabled = nphases >= 7
        S.add("scalar", _load_table_set(nc, (AF.Silu,)), writes=["tab5"])
        for half in range(2):
            for fc in range(32):
                ci = half * 32 + fc
                par = fc % 2
                gsl = w_fl[:, 0:2048] if par == 0 else w5B[:, 0:2048]
                usl = w_fl[:, 2048:4096] if par == 0 else w5B[:, 2048:4096]
                gres, ures = "wg%d" % par, "wu%d" % par
                dma("gpsimd", gsl.rearrange("p (a f) -> p a f", f=128),
                    gateT[ci], writes=[gres])
                dma("gpsimd", usl.rearrange("p (a f) -> p a f", f=128),
                    upT[ci], writes=[ures])
                for tc in range(2):
                    pg = psb[tc]
                    pu = psb[2 + tc]
                    for k in range(16):
                        S.add("tensor",
                              lambda e, k=k, tc=tc, pg=pg, gsl=gsl: e.matmul(
                                  pg[:, 0:512],
                                  gsl[:, k * 128:(k + 1) * 128],
                                  h2_fm[:, k, tc * 512:(tc + 1) * 512],
                                  start=(k == 0), stop=(k == 15)),
                              reads=[gres, "h2"], writes=["b%d" % tc])
                    for k in range(16):
                        S.add("tensor",
                              lambda e, k=k, tc=tc, pu=pu, usl=usl: e.matmul(
                                  pu[:, 0:512],
                                  usl[:, k * 128:(k + 1) * 128],
                                  h2_fm[:, k, tc * 512:(tc + 1) * 512],
                                  start=(k == 0), stop=(k == 15)),
                              reads=[ures, "h2"], writes=["b%d" % (2 + tc)])
                    sit = "sit%d" % tc
                    sdst = silu_t if tc == 0 else silu_t2
                    S.add("scalar",
                          lambda e, tc=tc, pg=pg, sdst=sdst: e.activation(
                              sdst, pg[:, 0:512], AF.Silu),
                          reads=["b%d" % tc, "tab5"], writes=[sit])
                    vop(lambda e, fc=fc, tc=tc, pu=pu, sdst=sdst: e.tensor_mul(
                        gated[:, fc, tc * 512:(tc + 1) * 512], sdst,
                        pu[:, 0:512]),
                        reads=[sit, "b%d" % (2 + tc)], writes=["gated"])
            if dump_sp and half == 0:
                dma("sync", dbg_gated, gated, reads=["gated"], writes=["dbgd"])
            for dc in range(4):
                dma("sync", bc_sb[:, 0, :],
                    bass.AP(rows_d.tensor, D + dc * 512, [[0, 128], [1, 512]]),
                    reads=["rows"], writes=["bc0"])
                for kp in range(8):
                    slot = (kp % 2) * 2048
                    wres = "wd%d" % (kp % 2)
                    dma("gpsimd", w_fl[:, slot:slot + 2048],
                        downT[half, dc, kp], writes=[wres])
                    for ot in range(NOWN):
                        pd = psb[ot] if ot < 7 else ptr_f32
                        for k in range(4):
                            kc32 = kp * 4 + k
                            S.add("tensor",
                                  lambda e, k=k, kc32=kc32, ot=ot, pd=pd,
                                  slot=slot: e.matmul(
                                      pd[:, 0:512],
                                      gated[:, kc32, ot * 128:(ot + 1) * 128],
                                      w_fl[:, slot + k * 512:
                                           slot + (k + 1) * 512],
                                      start=(kc32 == 0), stop=(kc32 == 31)),
                                  reads=[wres, "gated"],
                                  writes=["b%d" % min(ot, 7)])
                for ot in range(NOWN):
                    pd = psb[ot] if ot < 7 else ptr_f32
                    vop(lambda e, ot=ot, pd=pd: e.tensor_mul(
                        bc_sb[:, 1, :], pd[:, 0:512], bc_sb[:, 0, :]),
                        reads=["b%d" % min(ot, 7), "bc0"], writes=["bc1"])
                    vop(lambda e, ot=ot, dc=dc: e.tensor_add(
                        x2[:, ot, dc * 512:(dc + 1) * 512],
                        x2[:, ot, dc * 512:(dc + 1) * 512], bc_sb[:, 1, :]),
                        reads=["bc1"], writes=["x2"])

        S.enabled = True
        S.fence()
        outs = []
        for ot in range(NOWN):
            outs.append(dma("sync", y_own[ot * 128:(ot + 1) * 128, :],
                            x2[:, ot, :], reads=["x2"], writes=["y"]))
        S.add("gpsimd", None, extra=outs)

        S.finalize()
        with nc.Block() as blk:
            @blk.sync
            def _(eng):
                S.emit_for("sync", eng, sems)

            @blk.gpsimd
            def _(eng):
                S.emit_for("gpsimd", eng, sems)

            @blk.tensor
            def _(eng):
                S.emit_for("tensor", eng, sems)

            @blk.vector
            def _(eng):
                S.emit_for("vector", eng, sems, sep_ap=sepv[:])

            @blk.scalar
            def _(eng):
                S.emit_for("scalar", eng, sems, sep_ap=seps[:])

    return nc, len(S.ops)


_CACHE = {}


def _get_program():
    if "nc" not in _CACHE:
        _CACHE["nc"] = build()[0]
    return _CACHE["nc"]


def _rope_tables():
    d = HD
    inv = 1.0 / (10000.0 ** (np.arange(0, d, 2, dtype=np.float64) / d))
    t = np.arange(L, dtype=np.float64)
    fr = np.outer(inv, t)
    cos = np.cos(fr)
    sin = np.sin(fr)
    cos_full = np.concatenate([cos, cos], axis=0)
    sin_full = np.concatenate([sin, sin], axis=0)
    return cos_full.astype(np.float32), sin_full.astype(np.float32)


def kernel(L0_attn_hidden_states, L0_attn_wq, L0_attn_wk, L0_attn_wv,
           L0_attn_wg, L0_attn_wo, L0_attn_norm_w, L0_ffn_w_gate,
           L0_ffn_w_up, L0_ffn_w_down, L0_c, L0_wmod):
    x = np.ascontiguousarray(np.asarray(L0_attn_hidden_states, dtype=np.float32))
    wqT = np.ascontiguousarray(np.asarray(L0_attn_wq, np.float32).T)
    wkT = np.ascontiguousarray(np.asarray(L0_attn_wk, np.float32).T)
    wvT = np.ascontiguousarray(np.asarray(L0_attn_wv, np.float32).T)
    wgT = np.ascontiguousarray(np.asarray(L0_attn_wg, np.float32).T)
    woT = np.ascontiguousarray(np.asarray(L0_attn_wo, np.float32).T)
    gateT = np.ascontiguousarray(np.asarray(L0_ffn_w_gate, np.float32).T)
    upT = np.ascontiguousarray(np.asarray(L0_ffn_w_up, np.float32).T)
    downT = np.ascontiguousarray(np.asarray(L0_ffn_w_down, np.float32).T)
    wmod = np.ascontiguousarray(np.asarray(L0_wmod, np.float32))
    c = np.asarray(L0_c, np.float32)
    nw = np.asarray(L0_attn_norm_w, np.float32)

    nc = _get_program()
    in_maps = _make_in_maps(x, wqT, wkT, wvT, wgT, woT, gateT, upT, downT,
                            wmod, c, nw)

    res = run_bass_kernel_spmd(nc, in_maps, core_ids=list(range(8)))

    out = np.empty((B, L, D), dtype=np.float32)
    for core in range(8):
        b = core // 2
        r = core % 2
        y = res.results[core]["y_own"]
        for i in range(NOWN):
            g = 2 * i + r
            out[b, 128 * g:128 * g + 128, :] = y[128 * i:128 * i + 128, :]
    return out


def _tile_w2(wT):
    D_, F_ = wT.shape
    t = wT.reshape(16, 128, F_ // 256, 256)
    return np.ascontiguousarray(t.transpose(2, 1, 0, 3))


def _tile_g(wT):
    D_, F_ = wT.shape
    t = wT.reshape(16, 128, F_ // 128, 128)
    return np.ascontiguousarray(t.transpose(2, 1, 0, 3))


def _tile_d(dT):
    out = np.empty((2, 4, 8, 128, 2048), dtype=np.float32)
    for half in range(2):
        for dc in range(4):
            for kp in range(8):
                blk = dT[half * 4096 + kp * 512: half * 4096 + (kp + 1) * 512,
                         dc * 512:(dc + 1) * 512]
                out[half, dc, kp] = (blk.reshape(4, 128, 512)
                                     .transpose(1, 0, 2).reshape(128, 2048))
    return out


def _make_in_maps(x, wqT, wkT, wvT, wgT, woT, gateT, upT, downT, wmod, c, nw):
    cos_full, sin_full = _rope_tables()
    ident = np.eye(128, dtype=np.float32)
    pmat = np.zeros((128, 128), dtype=np.float32)
    for dd in range(64):
        pmat[dd + 64, dd] = -1.0
        pmat[dd, dd + 64] = 1.0
    nwrep = np.ascontiguousarray(np.broadcast_to(nw[None, :], (128, 128)))

    wq_t = _tile_w2(wqT)
    wk_t = _tile_w2(wkT)
    wv_t = _tile_w2(wvT)
    wg_t = _tile_w2(wgT)
    wo_t = _tile_w2(woT)
    gate_t = _tile_g(gateT)
    up_t = _tile_g(upT)
    down_t = _tile_d(downT)

    in_maps = []
    for core in range(8):
        b = core // 2
        r = core % 2
        own_tiles = [2 * i + r for i in range(NOWN)]
        own_idx = np.concatenate([np.arange(128 * g, 128 * g + 128)
                                  for g in own_tiles])
        perm = np.arange(L)
        if r == 1:
            perm = perm.reshape(NT // 2, 2, 128)[:, ::-1, :].reshape(L)
        inv_kpos = perm
        c_fm = np.ascontiguousarray(c[b].reshape(16, 128).T)
        masks = np.empty((NOWN, 128, 512), dtype=np.float32)
        for i, g in enumerate(own_tiles):
            kb = 512 * (kc_of(g) - 1)
            q_glob = 128 * g + np.arange(128)[:, None]
            k_prog = kb + np.arange(512)[None, :]
            k_glob = inv_kpos[np.minimum(k_prog, L - 1)]
            masks[i] = np.where(k_glob <= q_glob, 0.0, NEG)
        masks = np.ascontiguousarray(masks.transpose(1, 0, 2))
        in_maps.append({
            "x_kv": np.ascontiguousarray(x[b][inv_kpos]),
            "x_own": np.ascontiguousarray(x[b][own_idx]),
            "wqT": wq_t, "wkT": wk_t, "wvT": wv_t, "wgT": wg_t, "woT": wo_t,
            "gateT": gate_t, "upT": up_t, "downT": down_t, "wmod": wmod,
            "c_fm": c_fm, "normw_rep": nwrep,
            "cos_k": np.ascontiguousarray(cos_full[:, inv_kpos]),
            "sin_k": np.ascontiguousarray(sin_full[:, inv_kpos]),
            "cos_q": np.ascontiguousarray(cos_full[:, own_idx]),
            "sin_q": np.ascontiguousarray(sin_full[:, own_idx]),
            "masks": masks, "ident": ident, "pmat": pmat,
        })
    return in_maps


# version 09 (final)
# speedup vs baseline: 1.1743x; 1.0974x over previous
import math
from collections import defaultdict

import numpy as np

import concourse.bass as bass
import concourse.mybir as mybir
from concourse.bass_utils import run_bass_kernel_spmd

F32 = mybir.dt.float32
BF16 = mybir.dt.bfloat16
AF = mybir.ActivationFunctionType
ALU = mybir.AluOpType
AX = mybir.AxisListType

B, L, D = 4, 2048, 2048
NH, HD = 16, 128
DF = 8192
EPS = 1e-5
SCALE = HD ** -0.5
NT = L // 128
NOWN = 8
KC = [1, 1, 2, 2, 3, 3, 4, 4]
NEG = -1.0e30


class Op:
    __slots__ = ("eng", "fn", "deps", "inc", "idx", "after", "need", "qsnap")

    def __init__(self, eng, fn, deps, inc, idx):
        self.eng, self.fn, self.deps, self.inc, self.idx = eng, fn, deps, inc, idx
        self.after = 0
        self.need = inc == 16


class Res:
    __slots__ = ("w", "r")

    def __init__(self):
        self.w = None
        self.r = []


ENGS = ("sync", "gpsimd", "tensor", "vector", "scalar")


class Sched:
    def __init__(self):
        self.ops = []
        self.fence_deps = []
        self.R = defaultdict(Res)
        self.enabled = True

    def add(self, eng, fn, reads=(), writes=(), extra=(), dma=False):
        if not self.enabled:
            return None
        R = self.R
        deps = list(self.fence_deps)
        for name in reads:
            res = R[name]
            if res.w is not None:
                deps.append(res.w)
        for name in writes:
            res = R[name]
            if res.w is not None:
                deps.append(res.w)
            deps.extend(res.r)
        deps.extend(d for d in extra if d is not None)
        best = {}
        for d in deps:
            k = d.eng
            if k not in best or d.idx > best[k].idx:
                best[k] = d
        deps = list(best.values())
        op = Op(eng, fn, deps, 16 if dma else 1, len(self.ops))
        for d in deps:
            assert d.idx < op.idx
        self.ops.append(op)
        for name in reads:
            R[name].r.append(op)
        for name in writes:
            R[name].w = op
            R[name].r = []
        return op

    def fence(self):
        last = {}
        for op in self.ops:
            last[op.eng] = op
        self.fence_deps = list(last.values())

    def finalize(self):
        for op in self.ops:
            for d in op.deps:
                if d.eng != op.eng or d.inc == 16:
                    d.need = True
        cum = defaultdict(int)
        dma_cum = defaultdict(int)
        for op in self.ops:
            op.qsnap = dict(dma_cum)
            if op.need:
                cum[op.eng] += op.inc
                if op.inc == 16:
                    dma_cum[op.eng] = cum[op.eng]
            op.after = cum[op.eng]

    def emit_for(self, engname, eng, sems, sep_ap=None):
        hw = defaultdict(int)
        prev = None
        nsep = 0
        for op in self.ops:
            if op.eng != engname:
                continue
            for d in op.deps:
                if d.eng == engname and d.inc != 16:
                    continue
                val = op.qsnap.get(d.eng, 0) if d.inc == 16 else d.after
                assert val >= d.after or d.inc != 16
                if val > hw[d.eng]:
                    eng.wait_ge(sems[d.eng], val)
                    hw[d.eng] = val
            if engname == "gpsimd" and op.inc == 16:
                val = op.qsnap.get("gpsimd", 0)
                if val > hw["gpsimd"]:
                    eng.wait_ge(sems["gpsimd"], val)
                    hw["gpsimd"] = val
            if (sep_ap is not None and prev is not None
                    and any(d is prev for d in op.deps)):
                if engname == "scalar":
                    eng.mul(sep_ap, sep_ap, 1.0)
                else:
                    eng.tensor_scalar_mul(sep_ap, sep_ap, 1.0)
                nsep += 1
            if op.fn is not None:
                ins = op.fn(eng)
                if op.need:
                    ins.then_inc(sems[op.eng], op.inc)
                prev = op
        self.nsep = getattr(self, "nsep", 0) + nsep


def kc_of(g):
    return (g + 4) // 4


def _load_table_set(nc, want):
    from concourse.hw_specs import get_activation_tables
    for idx, funcs in enumerate(get_activation_tables(nc.m.arch).values()):
        if all(f in funcs for f in want):
            def emit(e, idx=idx):
                return e.add_instruction(mybir.InstLoadActFuncSet(
                    name=nc.get_next_instruction_name(),
                    act_func_set_id=idx, ins=[], outs=[]))
            return emit
    raise ValueError(f"no table set contains {want}")


def build(nphases=99, p2lvl=99, dump_sp=False, kinds="KVQG"):
    nc = bass.Bass("TRN2", target_bir_lowering=False,
                   detect_race_conditions=False)

    def din(name, shape):
        return nc.dram_tensor(name, shape, F32, kind="ExternalInput").ap()

    x_kv = din("x_kv", [L, D])
    x_own = din("x_own", [NOWN * 128, D])
    wqT = din("wqT", [8, 128, 16, 256])
    wkT = din("wkT", [8, 128, 16, 256])
    wvT = din("wvT", [8, 128, 16, 256])
    wgT = din("wgT", [8, 128, 16, 256])
    woT = din("woT", [8, 128, 16, 256])
    gateT = din("gateT", [64, 128, 16, 128])
    upT = din("upT", [64, 128, 16, 128])
    downT = din("downT", [2, 4, 8, 128, 2048])
    wmod = din("wmod", [D, 6 * D])
    c_fm = din("c_fm", [128, 16])
    nwrep_in = din("normw_rep", [128, 128])
    cos_k = din("cos_k", [128, L])
    sin_k = din("sin_k", [128, L])
    cos_q = din("cos_q", [128, NOWN * 128])
    sin_q = din("sin_q", [128, NOWN * 128])
    masks_in = din("masks", [128, NOWN, 512])
    id_in = din("ident", [128, 128])
    pmat_in = din("pmat", [128, 128])

    y_own = nc.dram_tensor("y_own", [NOWN * 128, D], F32,
                           kind="ExternalOutput").ap()
    spkind = "ExternalOutput" if dump_sp else "Internal"
    Vsp = nc.dram_tensor("Vsp", [NH, L, HD], BF16, kind=spkind).ap()
    Gsp = nc.dram_tensor("Gsp", [NH, NOWN * 128, HD], BF16, kind=spkind).ap()
    if dump_sp:
        dbg_x2 = nc.dram_tensor("dbg_x2", [NOWN * 128, D], F32,
                                kind="ExternalOutput").ap()
        dbg_h2 = nc.dram_tensor("dbg_h2", [128, 16, 1024], BF16,
                                kind="ExternalOutput").ap()
        dbg_gat = nc.dram_tensor("dbg_gat", [128, 16, 1024], BF16,
                                 kind="ExternalOutput").ap()
        dbg_gated = nc.dram_tensor("dbg_gated", [128, 32, 1024], BF16,
                                   kind="ExternalOutput").ap()
        dbg_h = nc.dram_tensor("dbg_h", [128, 16, 2048], BF16,
                               kind="ExternalOutput").ap()
        dbg_st = nc.dram_tensor("dbg_st", [128, 6, 16], F32,
                                kind="ExternalOutput").ap()
        dbg_K = nc.dram_tensor("dbg_K", [128, 16, 2048], BF16,
                               kind="ExternalOutput").ap()
        dbg_Q = nc.dram_tensor("dbg_Q", [128, 16, 1024], BF16,
                               kind="ExternalOutput").ap()
    rows_d = nc.dram_tensor("rows_d", [2, D], F32).ap()

    S = Sched()

    from contextlib import ExitStack
    with ExitStack() as ctx:
        def sb(name, shape, dt):
            return ctx.enter_context(nc.sbuf_tensor(name, shape, dt))

        arA = sb("arA", [128, 16, 2048], BF16)
        arB = sb("arB", [128, 16, 2048], BF16)
        arC = sb("arC", [128, 16, 1024], BF16)
        arD = sb("arD", [128, 16, 1024], BF16)
        w_sb = sb("w_sb", [128, 16, 256], BF16)
        bc_sb = sb("bc_sb", [128, 2, 512], F32)
        nwrep = sb("nwrep", [128, 128], F32)
        sm = sb("sm", [128, 128], F32)
        stats = sb("stats", [128, 6, 16], F32)
        stat2 = sb("stat2", [128, 3, 16], F32)
        modb = sb("modb", [128, 2, 96], F32)
        modc = sb("modc", [128, 2, 16], BF16)
        id_bf = sb("id_bf", [128, 128], BF16)
        pmat = sb("pmat_s", [128, 128], BF16)
        c_sb = sb("c_sb", [128, 16], F32)
        bias_sb = sb("bias_sb", [128, 1], F32)
        sepv = sb("sepv", [128, 1], F32)
        seps = sb("seps", [128, 1], F32)
        ps0 = ctx.enter_context(nc.psum_tensor("ps0", [128, 512], F32))
        ps1 = ctx.enter_context(nc.psum_tensor("ps1", [128, 512], F32))
        ps2 = ctx.enter_context(nc.psum_tensor("ps2", [128, 512], F32))
        ps3 = ctx.enter_context(nc.psum_tensor("ps3", [128, 512], F32))
        ps4 = ctx.enter_context(nc.psum_tensor("ps4", [128, 512], F32))
        ps5 = ctx.enter_context(nc.psum_tensor("ps5", [128, 512], F32))
        ps6 = ctx.enter_context(nc.psum_tensor("ps6", [128, 512], F32))
        ptr = ctx.enter_context(nc.psum_tensor("ptr", [128, 1024], BF16))
        sem_sync = ctx.enter_context(nc.semaphore("sem_sync"))
        sem_gps = ctx.enter_context(nc.semaphore("sem_gps"))
        sem_ten = ctx.enter_context(nc.semaphore("sem_ten"))
        sem_vec = ctx.enter_context(nc.semaphore("sem_vec"))
        sem_sca = ctx.enter_context(nc.semaphore("sem_sca"))
        sems = {"sync": sem_sync, "gpsimd": sem_gps, "tensor": sem_ten,
                "vector": sem_vec, "scalar": sem_sca}
        psb = [ps0, ps1, ps2, ps3, ps4, ps5, ps6]
        ptr_f32 = ptr[:].bitcast(F32)
        ps1_bf = ps1[:].bitcast(BF16)

        arA_fl = arA[:].rearrange("p a b -> p (a b)")
        arA_f32 = arA_fl.bitcast(F32)
        h_fm = arA
        wmod_f32 = arA_f32[:, 0:4096]
        wmod_f32b = arA_f32[:, 4096:8192]
        scores_sb = arA_f32[:, 0:2048]
        e_sb = arA_f32[:, 2048:4096]
        sg_sb = arA_f32[:, 4096:6144]
        av_sb = arA_f32[:, 6144:8192].rearrange("p (a b) -> p a b", b=128)
        p_sb = arA_fl[:, 16384:18432]
        p_sbB = arA_fl[:, 30208:32256]
        pT_sb = arA_fl[:, 18432:18944]
        pT_sb2 = arA_fl[:, 27648:28160]
        gtm_sb = arA_fl[:, 18944:19072]
        Gt_fl = arA_fl[:, 19456:21504]
        Gt = Gt_fl.rearrange("p (a b) -> p a b", b=128)
        v_fl = arA_fl[:, 21504:23552]
        v_sb = v_fl.rearrange("p (a b) -> p a b", b=128)
        v_fl2 = arA_fl[:, 28160:30208]
        v_sb2 = v_fl2.rearrange("p (a b) -> p a b", b=128)
        mask_fl = arA_fl[:, 23552:27648]
        mask_sb = mask_fl.rearrange("p (a b) -> p a b", b=512)
        x3_sb = arA_f32[:, 0:2048]
        gated = arA_fl.rearrange("p (a b) -> p a b", b=1024)

        K_fm = arB
        x2 = arB[:].rearrange("p a b -> p (a b)").bitcast(F32).rearrange(
            "p (a b) -> p a b", b=2048)
        Q_own = arC
        h2_fm = arC

        arD_fl = arD[:].rearrange("p a b -> p (a b)")
        arD_f32 = arD_fl.bitcast(F32)
        x_sb = arD_f32[:, 0:2048]
        x_sb2 = arD_f32[:, 5120:7168]
        trashD = arD_f32[:, 2048:4096]
        xn_sb = arD_fl[:, 8192:10240]
        cosk_sb = arD_fl[:, 0:2048]
        sink_sb = arD_fl[:, 2048:4096]
        cosq_sb = arD_fl[:, 4096:5120]
        sinq_sb = arD_fl[:, 5120:6144]
        a_rp = arD_f32[:, 3072:3584]
        b_rp = arD_f32[:, 3584:4096]
        kraw = arD_fl[:, 8192:8704]
        w2_sb = arD_fl[:, 8704:12800].rearrange("p (a b) -> p a b", b=256)
        gat_fm = arD
        silu_t = arD_f32[:, 2048:2560]
        w5B = arD_fl[:, 8192:12288]
        silu_t2 = arD_f32[:, 2560:3072]

        w_fl = w_sb[:].rearrange("p a b -> p (a b)")
        vstage = bc_sb[:].rearrange("p a b -> p (a b)").bitcast(BF16)

        hK = "hK"

        def dma(eng, out, in_, reads=(), writes=(), extra=()):
            return S.add(eng, lambda e, o=out, i=in_: e.dma_start(out=o, in_=i),
                         reads=reads, writes=writes, extra=extra, dma=True)

        def vop(fn, reads=(), writes=(), eng="vector", extra=()):
            return S.add(eng, fn, reads=reads, writes=writes, extra=extra)

        S.add("vector", lambda e: e.memset(sepv[:], 0.0))
        S.add("scalar", lambda e: e.memzero(seps[:]))
        dma("sync", c_sb[:], c_fm, writes=["c"])
        dma("sync", nwrep[:], nwrep_in, writes=["nw"])
        dma("gpsimd", id_bf[:], id_in, writes=["id"])
        dma("gpsimd", pmat[:], pmat_in, writes=["pm"])
        S.add("scalar", _load_table_set(nc, (AF.Exp, AF.Ln, AF.Square)),
              writes=["tab"])
        S.add("scalar", lambda e: e.activation(stats[:, 0, :], c_sb[:], AF.Exp,
                                               scale=-1.0),
              reads=["c", "tab"], writes=["st"])
        vop(lambda e: e.tensor_scalar_add(stats[:, 0, :], stats[:, 0, :], 1.0),
            reads=["st"], writes=["st"])
        vop(lambda e: e.reciprocal(stats[:, 1, :], stats[:, 0, :]),
            reads=["st"], writes=["st"])
        vop(lambda e: e.tensor_mul(c_sb[:], c_sb[:], stats[:, 1, :]),
            reads=["st", "c"], writes=["sc"])

        S.add("vector", lambda e: e.memset(bc_sb[0:1, 1, 0:1], 1.0),
              writes=["bcid"])
        rowtmp = bc_sb[0:1, 0, :]
        allb = [ps0, ps1, ps2, ps3, ps4, ps5, ps6, ptr_f32]
        allr = ["b0", "b1", "b2", "b3", "b4", "b5", "b6", "b7"]
        for third in range(3):
            for k in range(16):
                slab = wmod_f32 if k % 2 == 0 else wmod_f32b
                sres = "wmod%d" % (k % 2)
                dma("sync", slab,
                    wmod[k * 128:(k + 1) * 128,
                         third * 4096:(third + 1) * 4096],
                    writes=[sres])
                for jj in range(8):
                    S.add("tensor",
                          lambda e, k=k, jj=jj, slab=slab: e.matmul(
                              allb[jj][0:1, 0:512], c_sb[:, k:k + 1],
                              slab[:, jj * 512:(jj + 1) * 512],
                              start=(k == 0), stop=(k == 15),
                              skip_group_check=True),
                          reads=[sres, "sc"], writes=[allr[jj]])
            for jj in range(8):
                j = third * 8 + jj
                vop(lambda e, jj=jj: e.tensor_copy(rowtmp, allb[jj][0:1, 0:512]),
                    reads=[allr[jj]], writes=["rowt"])
                for q in range(4):
                    S.add("tensor",
                          lambda e, q=q: e.transpose(
                              ps0[0:128, 0:1],
                              rowtmp[0:1, q * 128:(q + 1) * 128],
                              bc_sb[0:1, 1, 0:1]),
                          reads=["rowt", "bcid"], writes=["b0"])
                    vop(lambda e, j=j, q=q: e.tensor_copy(
                        modb[:, 0, j * 4 + q:j * 4 + q + 1],
                        ps0[0:128, 0:1]),
                        reads=["b0"], writes=["mraw"])

        vop(lambda e: e.tensor_copy(modb[:, 1, 0:16], modb[:, 0, 0:16]),
            reads=["mraw"], writes=["mfm"])
        vop(lambda e: e.tensor_scalar_add(modb[:, 1, 16:32], modb[:, 0, 16:32], 1.0),
            reads=["mraw"], writes=["mfm"])
        vop(lambda e: e.tensor_copy(modb[:, 1, 48:64], modb[:, 0, 48:64]),
            reads=["mraw"], writes=["mfm"])
        vop(lambda e: e.tensor_scalar_add(modb[:, 1, 64:80], modb[:, 0, 64:80], 1.0),
            reads=["mraw"], writes=["mfm"])
        for seg in (32, 80):
            vop(lambda e, s=seg: e.tensor_scalar_max(stats[:, 0, :],
                                                     modb[:, 0, s:s + 16], 0.25),
                reads=["mraw"], writes=["st"])
            vop(lambda e, s=seg: e.tensor_scalar_min(stats[:, 1, :],
                                                     modb[:, 0, s:s + 16], -0.25),
                reads=["mraw"], writes=["st"])
            vop(lambda e, s=seg: e.tensor_scalar(stats[:, 2, :],
                                                 modb[:, 0, s:s + 16], 0.0, None,
                                                 ALU.is_ge),
                reads=["mraw"], writes=["st"])
            vop(lambda e: e.tensor_sub(stats[:, 3, :], stats[:, 0, :],
                                       stats[:, 1, :]),
                reads=["st"], writes=["st"])
            vop(lambda e: e.tensor_mul(stats[:, 3, :], stats[:, 3, :],
                                       stats[:, 2, :]),
                reads=["st"], writes=["st"])
            vop(lambda e, s=seg: e.tensor_add(modb[:, 1, s:s + 16],
                                              stats[:, 1, :], stats[:, 3, :]),
                reads=["st"], writes=["mfm"])
        for i, seg in enumerate((32, 80)):
            vop(lambda e, i=i, s=seg: e.tensor_copy(modc[:, i, :],
                                                    modb[:, 1, s:s + 16]),
                reads=["mfm"], writes=["modc"])
            S.add("tensor",
                  lambda e, i=i: e.transpose(ptr[0:16, 0:128], modc[:, i, :],
                                             id_bf[:]),
                  reads=["modc", "id"], writes=["b7"])
            vop(lambda e, i=i: e.tensor_copy(bc_sb[0:16, i, 0:128],
                                             ptr[0:16, 0:128]),
                reads=["b7"], writes=["garow"])
        dma("sync", rows_d[0, :], bc_sb[0:16, 0, 0:128], reads=["garow"], writes=["rows"])
        dma("sync", rows_d[1, :], bc_sb[0:16, 1, 0:128], reads=["garow"], writes=["rows"])

        def modcol(base, k):
            return modb[:, 1, base + k:base + k + 1]

        S.fence()

        def emit_ln(x_tile, x_res, bb, opc_base, sft_base, out_t, out_col, out_res):
            S.add("scalar",
                  lambda e, x_tile=x_tile, bb=bb: e.activation(
                      trashD, x_tile, AF.Square,
                      accum_out=stats[:, 1, bb:bb + 1]),
                  reads=[x_res], writes=["trD", "st"])
            vop(lambda e, x_tile=x_tile, bb=bb: e.reduce_sum(
                stats[:, 0, bb:bb + 1], x_tile, AX.X),
                reads=[x_res], writes=["st"])
            vop(lambda e, bb=bb: e.tensor_scalar_mul(
                stats[:, 2, bb:bb + 1], stats[:, 0, bb:bb + 1], 1.0 / D),
                reads=["st"], writes=["st"])
            vop(lambda e, bb=bb: e.tensor_mul(
                stats[:, 5, bb:bb + 1], stats[:, 2, bb:bb + 1],
                stats[:, 2, bb:bb + 1]),
                reads=["st"], writes=["st"])
            vop(lambda e, bb=bb: e.tensor_scalar(
                stats[:, 4, bb:bb + 1], stats[:, 1, bb:bb + 1], 1.0 / D, EPS,
                ALU.mult, ALU.add),
                reads=["st"], writes=["st"])
            vop(lambda e, bb=bb: e.tensor_sub(
                stats[:, 4, bb:bb + 1], stats[:, 4, bb:bb + 1],
                stats[:, 5, bb:bb + 1]),
                reads=["st"], writes=["st"])
            S.add("scalar",
                  lambda e, bb=bb: e.activation(stats[:, 5, bb:bb + 1],
                                                stats[:, 4, bb:bb + 1], AF.Ln),
                  reads=["st"], writes=["st"])
            S.add("scalar",
                  lambda e, bb=bb: e.activation(stats[:, 3, bb:bb + 1],
                                                stats[:, 5, bb:bb + 1], AF.Exp,
                                                scale=-0.5),
                  reads=["st"], writes=["st"])
            vop(lambda e, x_tile=x_tile, bb=bb: e.tensor_scalar(
                xn_sb, x_tile, stats[:, 2, bb:bb + 1], stats[:, 3, bb:bb + 1],
                ALU.subtract, ALU.mult),
                reads=[x_res, "st"], writes=["xn"])
            for half in range(2):
                for kk in range(8):
                    k = half * 8 + kk
                    S.add("tensor",
                          lambda e, k=k, kk=kk: e.transpose(
                              ptr[:, kk * 128:(kk + 1) * 128],
                              xn_sb[:, k * 128:(k + 1) * 128], id_bf[:]),
                          reads=["xn", "id"], writes=["b7"])
                for kk in range(8):
                    k = half * 8 + kk
                    vop(lambda e, k=k, kk=kk, out_t=out_t, out_col=out_col,
                        ob=opc_base, sb=sft_base: e.tensor_scalar(
                            out_t[:, k, out_col:out_col + 128],
                            ptr[:, kk * 128:(kk + 1) * 128],
                            modcol(ob, k), modcol(sb, k), ALU.mult, ALU.add),
                        reads=["b7", "mfm"], writes=[out_res])

        S.enabled = nphases >= 2
        for blk in range(NT):
            xa = x_sb if blk % 2 == 0 else x_sb2
            xr = "xs%d" % (blk % 2)
            dma("sync", xa, x_kv[blk * 128:(blk + 1) * 128, :], writes=[xr])
            emit_ln(xa, xr, blk % 16, 16, 0, h_fm, blk * 128, hK)

        S.fence()
        if dump_sp:
            dma("sync", dbg_h, arA[:], reads=[hK], writes=["dbghh"])
            dma("sync", dbg_st, stats[:], reads=["st"], writes=["dbgst"])

        S.enabled = nphases >= 3
        dma("gpsimd", cosk_sb, cos_k, writes=["tabs"])
        dma("gpsimd", sink_sb, sin_k, writes=["tabs"])
        dma("gpsimd", cosq_sb, cos_q, writes=["tabs"])
        dma("gpsimd", sinq_sb, sin_q, writes=["tabs"])

        hAP = h_fm[:]
        pstep_h = hAP.ap[0][0]

        def own_rhs(k, oc):
            off = k * 2048 + 1024 * oc
            return bass.AP(arA, off, [[pstep_h, 128], [256, 4], [1, 128]])

        def rope_evac(pbank, pres, cos_t, sin_t, col0, dst, wname):
            vop(lambda e, pbank=pbank: e.tensor_copy(kraw, pbank[:, 0:512]),
                reads=[pres], writes=["kraw"])
            vop(lambda e, cos_t=cos_t, col0=col0, pbank=pbank: e.tensor_mul(
                a_rp, pbank[:, 0:512], cos_t[:, col0:col0 + 512]),
                reads=[pres, "tabs"], writes=["arp"])
            S.add("tensor",
                  lambda e: e.matmul(ps2[:, 0:512], pmat[:], kraw,
                                     start=True, stop=True),
                  reads=["kraw", "pm"], writes=["b2"])
            vop(lambda e, sin_t=sin_t, col0=col0: e.tensor_mul(
                b_rp, ps2[:, 0:512], sin_t[:, col0:col0 + 512]),
                reads=["b2", "tabs"], writes=["brp"])
            vop(lambda e, dst=dst: e.tensor_add(dst, a_rp, b_rp),
                reads=["arp", "brp"], writes=[wname])

        wcnt = 0
        for wT, kind in ((wkT, "K"), (wvT, "V"), (wqT, "Q"), (wgT, "G")):
            if kind not in kinds:
                continue
            for fc in range(8):
                wcnt += 1
                wb = w_sb[:] if wcnt % 2 == 0 else w2_sb
                wres = "w%d" % (wcnt % 2)
                dma("gpsimd", wb, wT[fc], writes=[wres])
                if kind in ("K", "Q"):
                    for hl in range(2):
                        h = fc * 2 + hl
                        ncols = 4 if kind == "K" else 2
                        for tc in range(ncols):
                            par = (hl * 4 + tc) % 2
                            pbank = ps0 if par == 0 else ps4
                            pres = "b0" if par == 0 else "b4"
                            for k in range(16):
                                rhs = (h_fm[:, k, tc * 512:(tc + 1) * 512]
                                       if kind == "K" else own_rhs(k, tc))
                                S.add("tensor",
                                      lambda e, k=k, hl=hl, rhs=rhs,
                                      pbank=pbank, wb=wb: e.matmul(
                                          pbank[:, 0:512],
                                          wb[:, k, hl * 128:(hl + 1) * 128],
                                          rhs, start=(k == 0), stop=(k == 15)),
                                      reads=[wres, hK], writes=[pres])
                            if kind == "K":
                                rope_evac(pbank, pres, cosk_sb, sink_sb,
                                          tc * 512,
                                          K_fm[:, h, tc * 512:(tc + 1) * 512],
                                          "Kf")
                            else:
                                rope_evac(pbank, pres, cosq_sb, sinq_sb,
                                          tc * 512,
                                          Q_own[:, h, tc * 512:(tc + 1) * 512],
                                          "Qf")
                else:
                    nblk = 16 if kind == "V" else NOWN
                    spname = "Vsp" if kind == "V" else "Gsp"
                    dstT = Vsp if kind == "V" else Gsp
                    seqlen = L if kind == "V" else NOWN * 128
                    for tb in range(nblk):
                        lo = tb * 128 if kind == "V" else 256 * tb
                        pbank = ps0 if tb % 2 == 0 else ps4
                        pres = "b0" if tb % 2 == 0 else "b4"
                        for k in range(16):
                            S.add("tensor",
                                  lambda e, k=k, lo=lo, pbank=pbank,
                                  wb=wb: e.matmul(
                                      pbank[:, 0:256],
                                      bass.AP(arA, k * 2048 + lo,
                                              [[pstep_h, 128], [1, 128]]),
                                      wb[:, k, :], start=(k == 0),
                                      stop=(k == 15)),
                                  reads=[wres, hK], writes=[pres])
                        vop(lambda e, tb=tb, pbank=pbank: e.tensor_copy(
                            vstage[:, (tb % 4) * 512:(tb % 4) * 512 + 256],
                            pbank[:, 0:256]),
                            reads=[pres], writes=["vst"])
                        if tb % 4 == 3 or tb == nblk - 1:
                            nb = tb % 4 + 1
                            tb0 = tb - nb + 1
                            for hh in range(2):
                                dst_ap = bass.AP(
                                    dstT.tensor,
                                    (2 * fc + hh) * seqlen * 128 + tb0 * 128 * 128,
                                    [[128, 128], [128 * 128, nb], [1, 128]])
                                src_ap = bass.AP(
                                    vstage.tensor,
                                    vstage.offset + hh * 128,
                                    [list(vstage.ap[0]), [512, nb], [1, 128]])
                                dma("sync", dst_ap, src_ap, reads=["vst"],
                                    writes=[spname])

        S.fence()
        if dump_sp:
            dma("sync", dbg_K, arB[:], reads=["Kf"], writes=["dbgk"])
            dma("sync", dbg_Q, arC[:], reads=["Qf"], writes=["dbgq"])

        S.enabled = nphases >= 4
        dma("gpsimd", mask_fl, masks_in.rearrange("p a c -> p (a c)"),
            writes=["mask"])
        for i in range(NOWN):
            kc = KC[i]
            dma("sync", Gt_fl,
                bass.AP(Gsp.tensor, i * 128 * 128,
                        [[128, 128], [NOWN * 128 * 128, 16], [1, 128]]),
                reads=["Gsp"], writes=["Gt"])
            S.add("scalar", lambda e: e.activation(e_sb, Gt_fl, AF.Exp,
                                                   scale=-1.0),
                  reads=["Gt", "tab"], writes=["es"])
            vop(lambda e: e.tensor_scalar_add(e_sb, e_sb, 1.0),
                reads=["es"], writes=["es"])
            S.add("scalar", lambda e: e.activation(e_sb, e_sb, AF.Ln),
                  reads=["es"], writes=["es"])
            S.add("scalar", lambda e: e.activation(e_sb, e_sb, AF.Exp,
                                                   scale=-1.0),
                  reads=["es"], writes=["es"])
            vop(lambda e: e.tensor_mul(sg_sb, Gt_fl, e_sb),
                reads=["es", "Gt"], writes=["sg"])
            def stage_A(h):
                if p2lvl < 2:
                    return
                vfl = v_fl if h % 2 == 0 else v_fl2
                vres = "vv%d" % (h % 2)
                dma("sync", vfl[:, 0:kc * 512],
                    bass.AP(Vsp.tensor, h * L * 128,
                            [[128, 128], [128 * 128, kc * 4], [1, 128]]),
                    reads=["Vsp"], writes=[vres])
                pcur = p_sb if h % 2 == 0 else p_sbB
                pres0 = "pp" if h % 2 == 0 else "ppB"
                sb_l = [ps0, ps4, ps5, ps6]
                sr_l = ["b0", "b4", "b5", "b6"]
                for j in range(kc):
                    S.add("tensor",
                          lambda e, h=h, i=i, j=j: e.matmul(
                              sb_l[j][:, 0:512],
                              Q_own[:, h, i * 128:(i + 1) * 128],
                              K_fm[:, h, j * 512:(j + 1) * 512],
                              start=True, stop=True),
                          reads=["Qf", "Kf"], writes=[sr_l[j]])
                vop(lambda e, kc=kc, i=i: e.tensor_add(
                    sb_l[kc - 1][:, 0:512], sb_l[kc - 1][:, 0:512],
                    mask_sb[:, i, :]),
                    reads=[sr_l[kc - 1], "mask"], writes=[sr_l[kc - 1]])
                for j in range(kc):
                    dst = bias_sb[:] if j == 0 else stats[:, 5, 15:16]
                    vop(lambda e, j=j, dst=dst: e.reduce_max(
                        dst, sb_l[j][:, 0:512], AX.X, negate=True),
                        reads=[sr_l[j]], writes=["bi"])
                    if j > 0:
                        vop(lambda e: e.tensor_tensor(
                            bias_sb[:], bias_sb[:], stats[:, 5, 15:16],
                            ALU.min),
                            reads=["bi"], writes=["bi"])
                vop(lambda e: e.tensor_scalar_mul(bias_sb[:], bias_sb[:], SCALE),
                    reads=["bi"], writes=["bi"])
                for j in range(kc):
                    acc = (stat2[:, 0, h:h + 1] if kc == 1
                           else stats[:, j, 15:16])
                    S.add("scalar",
                          lambda e, j=j, pcur=pcur, acc=acc: e.activation(
                              pcur[:, j * 512:(j + 1) * 512],
                              sb_l[j][:, 0:512], AF.Exp, bias=bias_sb[:],
                              scale=SCALE, accum_out=acc),
                          reads=[sr_l[j], "bi", "tab"],
                          writes=[pres0, "l2" if kc == 1 else "lt%d" % j])
                if kc > 1:
                    vop(lambda e, h=h: e.tensor_add(stat2[:, 0, h:h + 1],
                                                    stats[:, 0, 15:16],
                                                    stats[:, 1, 15:16]),
                        reads=["lt0", "lt1"], writes=["l2"])
                    for j in range(2, kc):
                        vop(lambda e, j=j, h=h: e.tensor_add(
                            stat2[:, 0, h:h + 1], stat2[:, 0, h:h + 1],
                            stats[:, j, 15:16]),
                            reads=["lt%d" % j], writes=["l2"])

            def stage_B(h):
                if p2lvl < 5:
                    return
                pcur = p_sb if h % 2 == 0 else p_sbB
                pres0 = "pp" if h % 2 == 0 else "ppB"
                vsb = v_sb if h % 2 == 0 else v_sb2
                vres = "vv%d" % (h % 2)
                for j in range(kc):
                    tb_ap = ptr if j % 2 == 0 else ps1_bf
                    tb_res = "b7" if j % 2 == 0 else "b1"
                    pT_dst = pT_sb if j % 2 == 0 else pT_sb2
                    pT_res = "pT" if j % 2 == 0 else "pT2"
                    for s4 in range(4):
                        S.add("tensor",
                              lambda e, j=j, s4=s4, tb_ap=tb_ap,
                              pcur=pcur: e.transpose(
                                  tb_ap[:, s4 * 128:(s4 + 1) * 128],
                                  pcur[:, j * 512 + s4 * 128:
                                       j * 512 + (s4 + 1) * 128], id_bf[:]),
                              reads=[pres0, "id"], writes=[tb_res])
                    vop(lambda e, tb_ap=tb_ap, pT_dst=pT_dst: e.tensor_copy(
                        pT_dst, tb_ap[:, 0:512]),
                        reads=[tb_res], writes=[pT_res])
                    for s4 in range(4):
                        S.add("tensor",
                              lambda e, j=j, s4=s4, kc=kc, pT_dst=pT_dst,
                              vsb=vsb: e.matmul(
                                  ps2[:, 0:128],
                                  pT_dst[:, s4 * 128:(s4 + 1) * 128],
                                  vsb[:, j * 4 + s4, :],
                                  start=(j == 0 and s4 == 0),
                                  stop=(j == kc - 1 and s4 == 3)),
                              reads=[pT_res, vres], writes=["b2"])
                if p2lvl < 6:
                    return
                sq_op = S.add("scalar",
                              lambda e, h=h: e.activation(
                                  bc_sb[:, 1, 0:128], ps2[:, 0:128], AF.Square,
                                  accum_out=stat2[:, 1, h:h + 1]),
                              reads=["b2"], writes=["sm1", "sq2"])
                vop(lambda e, h=h: e.tensor_copy(av_sb[:, h, :], ps2[:, 0:128]),
                    reads=["b2"], writes=["av"], extra=[sq_op])

            for h in range(NH):
                stage_A(h)
                stage_B(h)
            if p2lvl < 7:
                continue
            vop(lambda e: e.tensor_mul(stats[:, 0, :], stat2[:, 0, :],
                                       stat2[:, 0, :]),
                reads=["l2"], writes=["st"])
            vop(lambda e: e.tensor_scalar_mul(stats[:, 1, :], stat2[:, 1, :],
                                              1.0 / HD),
                reads=["sq2"], writes=["st"])
            vop(lambda e: e.scalar_tensor_tensor(stats[:, 1, :], stats[:, 0, :],
                                                 EPS, stats[:, 1, :],
                                                 ALU.mult, ALU.add),
                reads=["st"], writes=["st"])
            S.add("scalar", lambda e: e.activation(stats[:, 0, :],
                                                   stats[:, 1, :], AF.Ln),
                  reads=["st"], writes=["st"])
            S.add("scalar", lambda e: e.activation(stat2[:, 2, :],
                                                   stats[:, 0, :], AF.Exp,
                                                   scale=-0.5),
                  reads=["st"], writes=["al"])
            for hq in range(4):
                for hl in range(4):
                    h = hq * 4 + hl
                    vop(lambda e, h=h: e.tensor_mul(
                        sm[:, :], sg_sb[:, h * 128:(h + 1) * 128], nwrep[:]),
                        reads=["sg", "nw"], writes=["sm0"])
                    vop(lambda e, h=h: e.scalar_tensor_tensor(
                        gtm_sb, av_sb[:, h, :], stat2[:, 2, h:h + 1],
                        sm[:, :], ALU.mult, ALU.mult),
                        reads=["av", "al", "sm0"], writes=["gtm"])
                    S.add("tensor",
                          lambda e, hl=hl: e.transpose(
                              ptr[:, hl * 128:(hl + 1) * 128], gtm_sb,
                              id_bf[:]),
                          reads=["gtm", "id"], writes=["b7"])
                vop(lambda e, hq=hq, i=i: e.tensor_copy(
                    bass.AP(arD, hq * 4 * 1024 + i * 128,
                            [[16 * 1024, 128], [1024, 4], [1, 128]]),
                    ptr[:, 0:512].rearrange("p (a b) -> p a b", b=128)),
                    reads=["b7"], writes=["gat"])

        S.fence()
        if dump_sp:
            dma("sync", dbg_gat, arD[:], reads=["gat"], writes=["dbgg"])

        S.enabled = nphases >= 5
        for dc in range(8):
            dma("sync", bc_sb[:, 0, 0:256],
                bass.AP(rows_d.tensor, dc * 256, [[0, 128], [1, 256]]),
                reads=["rows"], writes=["bc0"])
            dma("gpsimd", w_sb[:], woT[dc], writes=["w"])
            for ot in range(NOWN):
                pbank = ps0 if ot % 2 == 0 else ps4
                pres = "b0" if ot % 2 == 0 else "b4"
                for k in range(16):
                    S.add("tensor",
                          lambda e, k=k, ot=ot, pbank=pbank: e.matmul(
                              pbank[:, 0:256],
                              bass.AP(arD, k * 1024 + ot * 128,
                                      [[16 * 1024, 128], [1, 128]]),
                              w_sb[:, k, :], start=(k == 0), stop=(k == 15)),
                          reads=["w", "gat"], writes=[pres])
                vop(lambda e, ot=ot, dc=dc, pbank=pbank: e.tensor_mul(
                    x2[:, ot, dc * 256:(dc + 1) * 256], pbank[:, 0:256],
                    bc_sb[:, 0, 0:256]),
                    reads=[pres, "bc0"], writes=["x2"])
        for ot in range(NOWN):
            dma("sync", x3_sb, x_own[ot * 128:(ot + 1) * 128, :], writes=["x3"])
            vop(lambda e, ot=ot: e.tensor_add(x2[:, ot, :], x2[:, ot, :], x3_sb),
                reads=["x3"], writes=["x2"])
        S.fence()
        if dump_sp:
            for ot in range(NOWN):
                dma("sync", dbg_x2[ot * 128:(ot + 1) * 128, :], x2[:, ot, :],
                    reads=["x2"], writes=["dbgx"])

        S.enabled = nphases >= 6
        for ot in range(NOWN):
            emit_ln(x2[:, ot, :], "x2", ot, 64, 48, h2_fm, ot * 128, "h2")

        S.fence()

        if dump_sp:
            dma("sync", dbg_h2, arC[:], reads=["h2"], writes=["dbgh"])

        S.enabled = nphases >= 7
        S.add("scalar", _load_table_set(nc, (AF.Silu,)), writes=["tab5"])
        for half in range(2):
            for fc in range(32):
                ci = half * 32 + fc
                par = fc % 2
                gsl = w_fl[:, 0:2048] if par == 0 else w5B[:, 0:2048]
                usl = w_fl[:, 2048:4096] if par == 0 else w5B[:, 2048:4096]
                gres, ures = "wg%d" % par, "wu%d" % par
                dma("gpsimd", gsl.rearrange("p (a f) -> p a f", f=128),
                    gateT[ci], writes=[gres])
                dma("gpsimd", usl.rearrange("p (a f) -> p a f", f=128),
                    upT[ci], writes=[ures])
                for tc in range(2):
                    pg = psb[tc]
                    pu = psb[2 + tc]
                    for k in range(16):
                        S.add("tensor",
                              lambda e, k=k, tc=tc, pg=pg, gsl=gsl: e.matmul(
                                  pg[:, 0:512],
                                  gsl[:, k * 128:(k + 1) * 128],
                                  h2_fm[:, k, tc * 512:(tc + 1) * 512],
                                  start=(k == 0), stop=(k == 15)),
                              reads=[gres, "h2"], writes=["b%d" % tc])
                    for k in range(16):
                        S.add("tensor",
                              lambda e, k=k, tc=tc, pu=pu, usl=usl: e.matmul(
                                  pu[:, 0:512],
                                  usl[:, k * 128:(k + 1) * 128],
                                  h2_fm[:, k, tc * 512:(tc + 1) * 512],
                                  start=(k == 0), stop=(k == 15)),
                              reads=[ures, "h2"], writes=["b%d" % (2 + tc)])
                    sit = "sit%d" % tc
                    sdst = silu_t if tc == 0 else silu_t2
                    S.add("scalar",
                          lambda e, tc=tc, pg=pg, sdst=sdst: e.activation(
                              sdst, pg[:, 0:512], AF.Silu),
                          reads=["b%d" % tc, "tab5"], writes=[sit])
                    vop(lambda e, fc=fc, tc=tc, pu=pu, sdst=sdst: e.tensor_mul(
                        gated[:, fc, tc * 512:(tc + 1) * 512], sdst,
                        pu[:, 0:512]),
                        reads=[sit, "b%d" % (2 + tc)], writes=["gated"])
            if dump_sp and half == 0:
                dma("sync", dbg_gated, gated, reads=["gated"], writes=["dbgd"])
            for dc in range(4):
                dma("sync", bc_sb[:, 0, :],
                    bass.AP(rows_d.tensor, D + dc * 512, [[0, 128], [1, 512]]),
                    reads=["rows"], writes=["bc0"])
                for kp in range(8):
                    slot = (kp % 2) * 2048
                    wres = "wd%d" % (kp % 2)
                    dma("gpsimd", w_fl[:, slot:slot + 2048],
                        downT[half, dc, kp], writes=[wres])
                    for ot in range(NOWN):
                        pd = psb[ot] if ot < 7 else ptr_f32
                        for k in range(4):
                            kc32 = kp * 4 + k
                            S.add("tensor",
                                  lambda e, k=k, kc32=kc32, ot=ot, pd=pd,
                                  slot=slot: e.matmul(
                                      pd[:, 0:512],
                                      gated[:, kc32, ot * 128:(ot + 1) * 128],
                                      w_fl[:, slot + k * 512:
                                           slot + (k + 1) * 512],
                                      start=(kc32 == 0), stop=(kc32 == 31)),
                                  reads=[wres, "gated"],
                                  writes=["b%d" % min(ot, 7)])
                for ot in range(NOWN):
                    pd = psb[ot] if ot < 7 else ptr_f32
                    vop(lambda e, ot=ot, pd=pd: e.tensor_mul(
                        bc_sb[:, 1, :], pd[:, 0:512], bc_sb[:, 0, :]),
                        reads=["b%d" % min(ot, 7), "bc0"], writes=["bc1"])
                    vop(lambda e, ot=ot, dc=dc: e.tensor_add(
                        x2[:, ot, dc * 512:(dc + 1) * 512],
                        x2[:, ot, dc * 512:(dc + 1) * 512], bc_sb[:, 1, :]),
                        reads=["bc1"], writes=["x2"])

        S.enabled = True
        S.fence()
        outs = []
        for ot in range(NOWN):
            outs.append(dma("sync", y_own[ot * 128:(ot + 1) * 128, :],
                            x2[:, ot, :], reads=["x2"], writes=["y"]))
        S.add("gpsimd", None, extra=outs)

        S.finalize()
        with nc.Block() as blk:
            @blk.sync
            def _(eng):
                S.emit_for("sync", eng, sems)

            @blk.gpsimd
            def _(eng):
                S.emit_for("gpsimd", eng, sems)

            @blk.tensor
            def _(eng):
                S.emit_for("tensor", eng, sems)

            @blk.vector
            def _(eng):
                S.emit_for("vector", eng, sems, sep_ap=sepv[:])

            @blk.scalar
            def _(eng):
                S.emit_for("scalar", eng, sems, sep_ap=seps[:])

    return nc, len(S.ops)


_CACHE = {}


def _get_program():
    if "nc" not in _CACHE:
        _CACHE["nc"] = build()[0]
    return _CACHE["nc"]


def _rope_tables():
    d = HD
    inv = 1.0 / (10000.0 ** (np.arange(0, d, 2, dtype=np.float64) / d))
    t = np.arange(L, dtype=np.float64)
    fr = np.outer(inv, t)
    cos = np.cos(fr)
    sin = np.sin(fr)
    cos_full = np.concatenate([cos, cos], axis=0)
    sin_full = np.concatenate([sin, sin], axis=0)
    return cos_full.astype(np.float32), sin_full.astype(np.float32)


def kernel(L0_attn_hidden_states, L0_attn_wq, L0_attn_wk, L0_attn_wv,
           L0_attn_wg, L0_attn_wo, L0_attn_norm_w, L0_ffn_w_gate,
           L0_ffn_w_up, L0_ffn_w_down, L0_c, L0_wmod):
    x = np.ascontiguousarray(np.asarray(L0_attn_hidden_states, dtype=np.float32))
    wqT = np.ascontiguousarray(np.asarray(L0_attn_wq, np.float32).T)
    wkT = np.ascontiguousarray(np.asarray(L0_attn_wk, np.float32).T)
    wvT = np.ascontiguousarray(np.asarray(L0_attn_wv, np.float32).T)
    wgT = np.ascontiguousarray(np.asarray(L0_attn_wg, np.float32).T)
    woT = np.ascontiguousarray(np.asarray(L0_attn_wo, np.float32).T)
    gateT = np.ascontiguousarray(np.asarray(L0_ffn_w_gate, np.float32).T)
    upT = np.ascontiguousarray(np.asarray(L0_ffn_w_up, np.float32).T)
    downT = np.ascontiguousarray(np.asarray(L0_ffn_w_down, np.float32).T)
    wmod = np.ascontiguousarray(np.asarray(L0_wmod, np.float32))
    c = np.asarray(L0_c, np.float32)
    nw = np.asarray(L0_attn_norm_w, np.float32)

    nc = _get_program()
    in_maps = _make_in_maps(x, wqT, wkT, wvT, wgT, woT, gateT, upT, downT,
                            wmod, c, nw)

    res = run_bass_kernel_spmd(nc, in_maps, core_ids=list(range(8)))

    out = np.empty((B, L, D), dtype=np.float32)
    for core in range(8):
        b = core // 2
        r = core % 2
        y = res.results[core]["y_own"]
        for i in range(NOWN):
            g = 2 * i + r
            out[b, 128 * g:128 * g + 128, :] = y[128 * i:128 * i + 128, :]
    return out


def _tile_w2(wT):
    D_, F_ = wT.shape
    t = wT.reshape(16, 128, F_ // 256, 256)
    return np.ascontiguousarray(t.transpose(2, 1, 0, 3))


def _tile_g(wT):
    D_, F_ = wT.shape
    t = wT.reshape(16, 128, F_ // 128, 128)
    return np.ascontiguousarray(t.transpose(2, 1, 0, 3))


def _tile_d(dT):
    out = np.empty((2, 4, 8, 128, 2048), dtype=np.float32)
    for half in range(2):
        for dc in range(4):
            for kp in range(8):
                blk = dT[half * 4096 + kp * 512: half * 4096 + (kp + 1) * 512,
                         dc * 512:(dc + 1) * 512]
                out[half, dc, kp] = (blk.reshape(4, 128, 512)
                                     .transpose(1, 0, 2).reshape(128, 2048))
    return out


def _make_in_maps(x, wqT, wkT, wvT, wgT, woT, gateT, upT, downT, wmod, c, nw):
    cos_full, sin_full = _rope_tables()
    ident = np.eye(128, dtype=np.float32)
    pmat = np.zeros((128, 128), dtype=np.float32)
    for dd in range(64):
        pmat[dd + 64, dd] = -1.0
        pmat[dd, dd + 64] = 1.0
    nwrep = np.ascontiguousarray(np.broadcast_to(nw[None, :], (128, 128)))

    wq_t = _tile_w2(wqT)
    wk_t = _tile_w2(wkT)
    wv_t = _tile_w2(wvT)
    wg_t = _tile_w2(wgT)
    wo_t = _tile_w2(woT)
    gate_t = _tile_g(gateT)
    up_t = _tile_g(upT)
    down_t = _tile_d(downT)

    in_maps = []
    for core in range(8):
        b = core // 2
        r = core % 2
        own_tiles = [2 * i + r for i in range(NOWN)]
        own_idx = np.concatenate([np.arange(128 * g, 128 * g + 128)
                                  for g in own_tiles])
        perm = np.arange(L)
        if r == 1:
            perm = perm.reshape(NT // 2, 2, 128)[:, ::-1, :].reshape(L)
        inv_kpos = perm
        c_fm = np.ascontiguousarray(c[b].reshape(16, 128).T)
        masks = np.empty((NOWN, 128, 512), dtype=np.float32)
        for i, g in enumerate(own_tiles):
            kb = 512 * (kc_of(g) - 1)
            q_glob = 128 * g + np.arange(128)[:, None]
            k_prog = kb + np.arange(512)[None, :]
            k_glob = inv_kpos[np.minimum(k_prog, L - 1)]
            masks[i] = np.where(k_glob <= q_glob, 0.0, NEG)
        masks = np.ascontiguousarray(masks.transpose(1, 0, 2))
        in_maps.append({
            "x_kv": np.ascontiguousarray(x[b][inv_kpos]),
            "x_own": np.ascontiguousarray(x[b][own_idx]),
            "wqT": wq_t, "wkT": wk_t, "wvT": wv_t, "wgT": wg_t, "woT": wo_t,
            "gateT": gate_t, "upT": up_t, "downT": down_t, "wmod": wmod,
            "c_fm": c_fm, "normw_rep": nwrep,
            "cos_k": np.ascontiguousarray(cos_full[:, inv_kpos]),
            "sin_k": np.ascontiguousarray(sin_full[:, inv_kpos]),
            "cos_q": np.ascontiguousarray(cos_full[:, own_idx]),
            "sin_q": np.ascontiguousarray(sin_full[:, own_idx]),
            "masks": masks, "ident": ident, "pmat": pmat,
        })
    return in_maps
